# version 05
# baseline (speedup 1.0000x reference)
import numpy as np

import concourse.bass as bass
import concourse.mybir as mybir
from concourse.bass_utils import run_bass_kernel_spmd
from concourse.aot_env import allow_ici_all_to_all

F32 = mybir.dt.float32
BF16 = mybir.dt.bfloat16
I32 = mybir.dt.int32
U8 = mybir.dt.uint8
AF = mybir.ActivationFunctionType
ALU = mybir.AluOpType
AX = mybir.AxisListType

D = 2048
T = 1024
NT = 8
F = 8192
NS = 64
E = 8
CAP = 512.0
RQ = 1024
RK = 512
NH = 8
EPS = 1e-5
QSCALE = 1.0 / np.sqrt(128.0)

N_SUBLAYERS = 3
DEBUG_ROUTER = False

ENGINES = ("tensor", "vector", "scalar", "gpsimd", "sync")
SAME_ENG_SYNC = {"vector": True, "scalar": True, "gpsimd": True,
                 "tensor": False, "sync": False}
DMA_LANES = {"sync": 6, "gpsimd": 6, "scalar": 2, "vector": 1, "tensor": 1}


class _Op:
    __slots__ = ("eng", "fn", "dma", "deps", "needed", "seq", "waits",
                 "vc_after", "tag", "lane")

    def __init__(self, eng, fn, dma, deps, tag):
        self.eng = eng
        self.fn = fn
        self.dma = dma
        self.deps = deps
        self.needed = False
        self.seq = None
        self.waits = ()
        self.vc_after = None
        self.tag = tag
        self.lane = None


class EM:

    def __init__(self):
        self.plan = []
        self.last_writer = {}
        self.readers = {}
        self.lane_rr = {e: 0 for e in ENGINES}
        self.lane_last = {}

    def op(self, eng, fn, r=(), w=(), dma=False, tag=""):
        deps = {}
        for k in r:
            d = self.last_writer.get(k)
            if d is not None:
                deps[id(d)] = d
        for k in w:
            d = self.last_writer.get(k)
            if d is not None:
                deps[id(d)] = d
            for rd in self.readers.get(k, ()):
                deps[id(rd)] = rd
        o = _Op(eng, fn, dma, list(deps.values()), tag)
        if dma:
            lane = self.lane_rr[eng] % DMA_LANES[eng]
            self.lane_rr[eng] += 1
            o.lane = lane
            prev = self.lane_last.get((eng, lane))
            if prev is not None:
                deps.setdefault(id(prev), prev)
                o.deps = list(deps.values())
            self.lane_last[(eng, lane)] = o
        for k in r:
            self.readers.setdefault(k, []).append(o)
        for k in w:
            self.last_writer[k] = o
            self.readers[k] = []
        self.plan.append(o)
        return o

    def finalize(self, nc):
        for o in self.plan:
            for d in o.deps:
                if d.dma:
                    continue
                if d.eng == o.eng and not SAME_ENG_SYNC[o.eng]:
                    continue
                d.needed = True
        ccount = {e: 0 for e in ENGINES}
        dcount = {}
        for o in self.plan:
            if o.dma:
                lk = (o.eng, o.lane)
                dcount[lk] = dcount.get(lk, 0) + 1
                o.seq = 16 * dcount[lk]
                o.needed = True
            elif o.needed:
                ccount[o.eng] += 1
                o.seq = ccount[o.eng]
        vclock = {e: {} for e in ENGINES}
        for o in self.plan:
            vc = vclock[o.eng]
            waits = {}
            for d in o.deps:
                same = (d.eng == o.eng) and not d.dma
                if d.vc_after:
                    for s2, v2 in d.vc_after.items():
                        if vc.get(s2, 0) < v2:
                            vc[s2] = v2
                if same and not SAME_ENG_SYNC[o.eng]:
                    continue
                stream = ("d", d.eng, d.lane) if d.dma else ("c", d.eng)
                v = d.seq
                assert v is not None, f"dep without seq: {d.tag} <- {o.tag}"
                if vc.get(stream, 0) < v and waits.get(stream, 0) < v:
                    waits[stream] = v
            for stream, v in waits.items():
                vc[stream] = max(vc.get(stream, 0), v)
            o.waits = sorted(waits.items())
            o.vc_after = dict(vc)
        self.final_dma = {(e, l): 16 * n for (e, l), n in dcount.items() if n}
        self.final_comp = {e: n for e, n in ccount.items() if n}

        sems = {}
        ctxs = []
        for e in ENGINES:
            s = nc.semaphore(f"c_{e}")
            sems[("c", e)] = s.__enter__()
            ctxs.append(s)
            for l in range(DMA_LANES[e]):
                if any((e2, l2) == (e, l) for (e2, l2) in dcount) or True:
                    s = nc.semaphore(f"d_{e}{l}")
                    sems[("d", e, l)] = s.__enter__()
                    ctxs.append(s)
        blk = nc.Block()
        block = blk.__enter__()
        per_eng = {e: [] for e in ENGINES}
        for o in self.plan:
            per_eng[o.eng].append(o)

        def mk(eng_name):
            ops = per_eng[eng_name]

            def body(eng):
                for o in ops:
                    for stream, v in o.waits:
                        eng.wait_ge(sems[stream], v)
                    ins = o.fn(eng)
                    if o.tag:
                        ins.annotate(o.tag)
                    if o.needed:
                        if o.dma:
                            ins.then_inc(sems[("d", o.eng, o.lane)], 16)
                        else:
                            ins.then_inc(sems[("c", o.eng)], 1)
                if eng_name == "sync":
                    for (e2, l2), v in self.final_dma.items():
                        eng.wait_ge(sems[("d", e2, l2)], v)
                    for e2, v in self.final_comp.items():
                        if e2 != "sync":
                            eng.wait_ge(sems[("c", e2)], v)
            return body

        block.tensor(mk("tensor"))
        block.vector(mk("vector"))
        block.scalar(mk("scalar"))
        block.gpsimd(mk("gpsimd"))
        block.sync(mk("sync"))
        blk.__exit__(None, None, None)
        for s in ctxs:
            s.__exit__(None, None, None)


def _gk(name, lo, hi, gran):
    return [(name, g) for g in range(lo // gran, (hi + gran - 1) // gran)]


def build_program(n_sub=3):
    nc = bass.Bass("TRN2", target_bir_lowering=False)
    em = EM()

    xin = nc.dram_tensor("x", [T, D], F32, kind="ExternalInput").ap()
    w1s = nc.dram_tensor("w1s", [NS, 128, 16, 128], F32, kind="ExternalInput").ap()
    w2s = nc.dram_tensor("w2s", [4, F, 512], F32, kind="ExternalInput").ap()
    b1s = nc.dram_tensor("b1s", [128, 64], F32, kind="ExternalInput").ap()
    b2row = nc.dram_tensor("b2row", [1, D], F32, kind="ExternalInput").ap()
    rTin = nc.dram_tensor("rT", [128, 128], F32, kind="ExternalInput").ap()
    selin = nc.dram_tensor("selmats", [64, 16], F32, kind="ExternalInput").ap()
    wqds = nc.dram_tensor("wqds", [8, 128, 16, 128], F32, kind="ExternalInput").ap()
    wkvds = nc.dram_tensor("wkvds", [4, 128, 16, 128], F32, kind="ExternalInput").ap()
    wqkv = nc.dram_tensor("wqkv", [NH, 128, 2048], F32, kind="ExternalInput").ap()
    wos = nc.dram_tensor("wos", [16, 128, 2048], F32, kind="ExternalInput").ap()
    borow = nc.dram_tensor("borow", [1, D], F32, kind="ExternalInput").ap()
    ctxidx_in = nc.dram_tensor("ctxidx_h", [128, 16], I32, kind="ExternalInput").ap()
    yout = nc.dram_tensor("y", [T, D], F32, kind="ExternalOutput").ap()
    dbg = nc.dram_tensor("dbg", [128, NT * 8], F32, kind="ExternalOutput").ap() \
        if DEBUG_ROUTER else None
    dbg2 = nc.dram_tensor("dbg2", [128, 272], F32, kind="ExternalOutput").ap() \
        if DEBUG_ROUTER else None
    dbg3 = nc.dram_tensor("dbg3", [128, 24576], BF16, kind="ExternalOutput").ap() \
        if DEBUG_ROUTER else None

    cc_in = [nc.dram_tensor(f"cc_in{i}", [E, 2], F32) for i in range(2)]
    cc_out = [nc.dram_tensor(f"cc_out{i}", [8 * E, 2], F32) for i in range(2)]
    lat_in = nc.dram_tensor("lat_in", [1536, 1024], BF16)
    lat_outq = nc.dram_tensor("lat_outq", [2048, 1024], BF16)
    lat_outk = nc.dram_tensor("lat_outk", [1024, 1024], BF16)
    ctx_in = nc.dram_tensor("ctx_in", [2048, 1024], BF16)
    ctx_out = nc.dram_tensor("ctx_out", [16384, 1024], BF16)
    w1bf = nc.dram_tensor("w1bf", [NS, 128, 2048], BF16)
    w2bf = nc.dram_tensor("w2bf", [4, NS, 128, 512], BF16)

    import contextlib
    stack = contextlib.ExitStack()

    def sb(name, shape, dtype):
        return stack.enter_context(nc.sbuf_tensor(name, shape, dtype))

    def psum(name, shape, dtype=F32):
        return stack.enter_context(nc.psum_tensor(name, shape, dtype))

    resid = sb("resid", [128, NT * D], F32)
    xnfm = sb("xnfm", [128, 16 * 1024], BF16)
    big = sb("big", [128, 64 * 512], BF16)
    xn32 = sb("xn32", [128, D], F32)
    hi_t = sb("hi_t", [128, D], BF16)
    lofm = sb("lofm", [128, D], BF16)
    wb = [sb("wb0", [128, 2048], BF16), sb("wb1", [128, 2048], BF16)]
    rt32 = sb("rt32", [128, 128], F32)
    rthi = sb("rthi", [128, 128], BF16)
    rtlo = sb("rtlo", [128, 128], BF16)
    b1sb = sb("b1sb", [128, 64], F32)
    b2sb = sb("b2sb", [1, D], BF16)
    bosb = sb("bosb", [1, D], BF16)
    ones1b = sb("ones1b", [1, 128], BF16)
    selm = sb("selm", [64, 16], F32)
    gath = sb("gath", [64, 2], F32)
    cntsb = sb("cntsb", [E, 2], F32)
    cnt16 = sb("cnt16", [E, 16], F32)
    cntT = sb("cntT", [8, 16], F32)
    comb = sb("comb", [E, 16], F32)
    bsb = sb("bsb", [E, 2], F32)
    b0row = sb("b0row", [1, E], F32)
    b1row = sb("b1row", [1, E], F32)
    identf = sb("identf", [128, 128], F32)
    identb = sb("identb", [128, 128], BF16)
    u128 = sb("u128", [128, 128], F32)
    cmask = sb("cmask", [128, 128], F32)
    ioti = sb("ioti", [128, 8], I32)
    ctxidx = sb("ctxidx", [128, 16], I32)
    iota8 = sb("iota8", [128, 8], F32)
    c8 = sb("c8", [128, 8], F32)
    ones1 = sb("ones1", [1, 128], F32)
    zrow = sb("zrow", [1, 128], BF16)
    onescol = sb("onescol", [128, 1], F32)
    c_eps = sb("c_eps", [128, 1], F32)
    c_mhalf = sb("c_mhalf", [128, 1], F32)
    c_qs = sb("c_qs", [128, 1], F32)
    sums = sb("sums", [128, NT], F32)
    sumsq = sb("sumsq", [128, NT], F32)
    mcol = sb("mcol", [128, NT], F32)
    rstd = sb("rstd", [128, NT], F32)
    vcol = sb("vcol", [128, NT], F32)
    nmr = sb("nmr", [128, NT], F32)
    accs = sb("accs", [128, NT * 4], F32)
    probs8 = sb("probs8", [128, 8], F32)
    pm8 = sb("pm8", [128, 8], F32)
    tmp8 = sb("tmp8", [128, 8], F32)
    msk8 = sb("msk8", [128, 8], U8)
    oh_all = sb("oh_all", [128, NT * 16], F32)
    pv_all = sb("pv_all", [128, NT * 2], F32)
    se_all = sb("se_all", [128, NT], F32)
    scol = sb("scol", [128, NT], F32)
    sc1 = sb("sc1", [128, 1], F32)
    sc2 = sb("sc2", [128, 1], F32)
    sc3 = sb("sc3", [128, 1], F32)
    dbgsb = sb("dbgsb", [128, NT * 8], F32) if DEBUG_ROUTER else None
    dbgsb2 = sb("dbgsb2", [128, 272], F32) if DEBUG_ROUTER else None
    dbgcols = sb("dbgcols", [128, 16], F32) if DEBUG_ROUTER else None
    dbgctx = sb("dbgctx", [128, 128], F32) if DEBUG_ROUTER else None
    idx1 = sb("idx1", [128, 1], F32)
    idx2 = sb("idx2", [128, 1], F32)
    mx4 = sb("mx4", [128, 4], F32)
    mx4b = sb("mx4b", [128, 4], F32)
    nmx = sb("nmx", [128, 1], F32)
    nmxb = sb("nmxb", [128, 1], F32)
    sume = sb("sume", [128, 4], F32)
    sumeb = sb("sumeb", [128, 4], F32)
    se1 = sb("se1", [128, 1], F32)
    se1b = sb("se1b", [128, 1], F32)
    rc1 = sb("rc1", [128, 1], F32)
    rc1b = sb("rc1b", [128, 1], F32)
    rbcb = sb("rbcb", [128, 128], F32)
    rrow = sb("rrow", [1, 128], F32)
    rbc = sb("rbc", [128, 128], F32)
    attnT = sb("attnT", [128, 512], BF16)

    psS = psum("psS", [128, 2048])
    psE = psum("psE", [128, 512])
    psF = psum("psF", [128, 512])
    psG = psum("psG", [128, 512])
    psT = psum("psT", [128, 1024], BF16)
    psS0 = psS[:, 0:512]
    psS1 = psS[:, 512:1024]
    psS2_ = psS[:, 1024:1536]
    psS3 = psS[:, 1536:2048]
    PS = [psS0, psS1, psS2_, psS3, psE, psF, psG]
    PSK = ["psS0", "psS1", "psS2", "psS3", "psE", "psF", "psG"]

    RG = 2048
    XG = 1024
    BG = 1024
    LOK = [("lo_t",)]
    HIK = [("hi_t",)]

    def P0():
        em.op("gpsimd", lambda e: e.memset(identf[:, :], 0.0), w=[("identf",)])
        em.op("gpsimd", lambda e: e.affine_select(
            out=identf[:, :], in_=identf[:, :], compare_op=ALU.not_equal,
            fill=1.0, base=0, pattern=[[-1, 128]], channel_multiplier=1),
            r=[("identf",)], w=[("identf",)])
        em.op("gpsimd", lambda e: e.memset(identb[:, :], 0.0), w=[("identb",)])
        em.op("gpsimd", lambda e: e.affine_select(
            out=identb[:, :], in_=identb[:, :], compare_op=ALU.not_equal,
            fill=1.0, base=0, pattern=[[-1, 128]], channel_multiplier=1),
            r=[("identb",)], w=[("identb",)])
        em.op("gpsimd", lambda e: e.memset(u128[:, :], 0.0), w=[("u128",)])
        em.op("gpsimd", lambda e: e.affine_select(
            out=u128[:, :], in_=u128[:, :], compare_op=ALU.is_ge,
            fill=1.0, base=0, pattern=[[-1, 128]], channel_multiplier=1),
            r=[("u128",)], w=[("u128",)])
        em.op("gpsimd", lambda e: e.memset(cmask[:, :], 0.0), w=[("cmask",)])
        em.op("gpsimd", lambda e: e.affine_select(
            out=cmask[:, :], in_=cmask[:, :], compare_op=ALU.is_ge,
            fill=-1e30, base=0, pattern=[[-1, 128]], channel_multiplier=1),
            r=[("cmask",)], w=[("cmask",)])
        em.op("gpsimd", lambda e: e.iota(ioti[:, :], pattern=[[1, 8]], base=0,
                                         channel_multiplier=0), w=[("ioti",)])
        em.op("vector", lambda e: e.tensor_copy(iota8[:, :], ioti[:, :]),
              r=[("ioti",)], w=[("iota8",)])
        em.op("gpsimd", lambda e: e.memset(c8[:, :], 8.0), w=[("c8",)])
        em.op("gpsimd", lambda e: e.memset(ones1[:, :], 1.0), w=[("ones1",)])
        em.op("gpsimd", lambda e: e.memset(zrow[:, :], 0.0), w=[("zrow",)])
        em.op("gpsimd", lambda e: e.memset(ones1b[:, :], 1.0), w=[("ones1b",)])
        em.op("gpsimd", lambda e: e.memset(onescol[:, :], 1.0), w=[("onescol",)])
        em.op("gpsimd", lambda e: e.memset(c_eps[:, :], EPS), w=[("c_eps",)])
        em.op("gpsimd", lambda e: e.memset(c_mhalf[:, :], -0.5), w=[("c_mhalf",)])
        em.op("gpsimd", lambda e: e.memset(c_qs[:, :], QSCALE), w=[("c_qs",)])
        em.op("sync", lambda e: e.dma_start(out=b1sb[:, :], in_=b1s[:, :]),
              dma=True, w=[("b1sb",)])
        em.op("gpsimd", lambda e: e.dma_start(out=b2sb[:, :], in_=b2row[:, :]),
              dma=True, w=[("b2sb",)])
        em.op("gpsimd", lambda e: e.dma_start(out=bosb[:, :], in_=borow[:, :]),
              dma=True, w=[("bosb",)])
        em.op("sync", lambda e: e.dma_start(out=selm[:, :], in_=selin[:, :]),
              dma=True, w=[("selm",)])
        em.op("sync", lambda e: e.dma_start(out=rt32[:, :], in_=rTin[:, :]),
              dma=True, w=[("rt32",)])
        em.op("sync", lambda e: e.dma_start(out=ctxidx[:, :], in_=ctxidx_in[:, :]),
              dma=True, w=[("ctxidx",)])
        em.op("vector", lambda e: e.tensor_copy(rthi[:, :], rt32[:, :]),
              r=[("rt32",)], w=[("rthi",)])
        em.op("vector", lambda e: e.tensor_tensor(rtlo[:, :], rt32[:, :],
                                                  rthi[:, :], ALU.subtract),
              r=[("rt32",), ("rthi",)], w=[("rtlo",)])
        for j in range(NT):
            em.op("sync", lambda e, j=j: e.dma_start(
                out=resid[:, j * D:(j + 1) * D], in_=xin[j * 128:(j + 1) * 128, :]),
                dma=True, w=[("resid", j)])
            em.op("vector", lambda e, j=j: e.reduce_sum(
                sums[:, j:j + 1], resid[:, j * D:(j + 1) * D], axis=AX.X),
                r=[("resid", j)], w=[("sums", j)])

    def LN_STATS(j, sum_from_accs, tag):
        rk = ("resid", j)
        if sum_from_accs:
            em.op("vector", lambda e: e.reduce_sum(
                sums[:, j:j + 1], accs[:, j * 4:(j + 1) * 4], axis=AX.X),
                r=[("accs", j)], w=[("sums", j)], tag=f"{tag}sum{j}")
        em.op("scalar", lambda e: e.activation(
            out=xn32[:, :], in_=resid[:, j * D:(j + 1) * D], func=AF.Square,
            accum_out=sumsq[:, j:j + 1]),
            r=[rk], w=[("xn32",), ("sumsq", j)], tag=f"{tag}sq{j}")
        em.op("vector", lambda e: e.tensor_scalar(
            mcol[:, j:j + 1], sums[:, j:j + 1], 1.0 / D, None, ALU.mult),
            r=[("sums", j)], w=[("mcol", j)])
        em.op("vector", lambda e: e.tensor_scalar(
            vcol[:, j:j + 1], sumsq[:, j:j + 1], 1.0 / D, None, ALU.mult),
            r=[("sumsq", j)], w=[("vcol", j)])
        em.op("vector", lambda e: e.tensor_scalar(
            sc1[:, :], mcol[:, j:j + 1], mcol[:, j:j + 1], None, ALU.mult),
            r=[("mcol", j)], w=[("sc1",)])
        em.op("vector", lambda e: e.tensor_tensor(
            vcol[:, j:j + 1], vcol[:, j:j + 1], sc1[:, :], ALU.subtract),
            r=[("vcol", j), ("sc1",)], w=[("vcol", j)], tag=f"{tag}var{j}")

    def LN_RSTD(js, tag):
        j0, j1 = js[0], js[-1] + 1
        rr = [("vcol", j) for j in js]
        em.op("scalar", lambda e: e.activation(
            out=rstd[:, j0:j1], in_=vcol[:, j0:j1], func=AF.Ln,
            bias=c_eps[:, :]),
            r=rr + [("c_eps",)], w=[("rstd_raw", j0)], tag=f"{tag}ln")
        em.op("scalar", lambda e: e.activation(
            out=rstd[:, j0:j1], in_=rstd[:, j0:j1], func=AF.Exp,
            scale=c_mhalf[:, :]),
            r=[("rstd_raw", j0), ("c_mhalf",)],
            w=[("rstd", j) for j in js], tag=f"{tag}exp")
        for j in js:
            em.op("vector", lambda e, j=j: e.tensor_scalar(
                nmr[:, j:j + 1], mcol[:, j:j + 1], rstd[:, j:j + 1], -1.0,
                ALU.mult, ALU.mult),
                r=[("mcol", j), ("rstd", j)], w=[("nmr", j)])

    def LN_NORM(j, want_lo, tag):
        rk = ("resid", j)
        em.op("vector", lambda e: e.tensor_scalar(
            xn32[:, :], resid[:, j * D:(j + 1) * D],
            rstd[:, j:j + 1], nmr[:, j:j + 1], ALU.mult, ALU.add),
            r=[rk, ("rstd", j), ("nmr", j), ("xn32",)], w=[("xn32",)],
            tag=f"{tag}norm{j}")
        em.op("vector", lambda e: e.tensor_copy(hi_t[:, :], xn32[:, :]),
              r=[("xn32",)], w=HIK)
        if want_lo:
            em.op("vector", lambda e: e.tensor_tensor(
                lofm[:, :], xn32[:, :], hi_t[:, :], ALU.subtract),
                r=[("xn32",)] + HIK, w=LOK)
        for half in range(2):
            for cc in range(8):
                c = half * 8 + cc
                em.op("tensor", lambda e, c=c, cc=cc: e.transpose(
                    psT[:, cc * 128:(cc + 1) * 128],
                    hi_t[:, c * 128:(c + 1) * 128], identb[:, :]),
                    r=HIK + [("identb",)], w=[("psT",)])
            xv = xnfm[:, :].rearrange("p (c t) -> p c t", c=16)
            em.op("vector", lambda e, half=half: e.tensor_copy(
                xv[:, half * 8:(half + 1) * 8, j * 128:(j + 1) * 128],
                psT[:, :].rearrange("p (c t) -> p c t", c=8)),
                r=[("psT",)],
                w=_gk("xnfm", half * 8 * XG, (half + 1) * 8 * XG, XG))
        if want_lo:
            for half in range(2):
                for cc in range(8):
                    c = half * 8 + cc
                    em.op("tensor", lambda e, c=c, cc=cc: e.transpose(
                        psT[:, cc * 128:(cc + 1) * 128],
                        lofm[:, c * 128:(c + 1) * 128], identb[:, :]),
                        r=LOK + [("identb",)], w=[("psT",)])
                em.op("vector", lambda e, half=half: e.tensor_copy(
                    lofm[:, half * 1024:(half + 1) * 1024], psT[:, :]),
                    r=[("psT",)], w=LOK)

    def ROUTER_TILE(j, tag):
        logit_ps = psF
        hkeys = _gk("xnfm", 0, 16 * XG, XG)
        nmm = 16 * 3
        k = 0
        for c in range(16):
            hi_ap = xnfm[:, c * 1024 + j * 128: c * 1024 + (j + 1) * 128]
            lo_ap = lofm[:, c * 128:(c + 1) * 128]
            rhi = rthi[:, c * 8:(c + 1) * 8]
            rlo = rtlo[:, c * 8:(c + 1) * 8]
            for lhs_ap, rhs_ap, lk, rk2 in (
                    (hi_ap, rhi, hkeys[c * 1024 // XG], ("rthi",)),
                    (hi_ap, rlo, hkeys[c * 1024 // XG], ("rtlo",)),
                    (lo_ap, rhi, LOK[0], ("rthi",))):
                em.op("tensor", lambda e, l=lhs_ap, rr=rhs_ap, s=(k == 0),
                      p=(k == nmm - 1): e.matmul(
                          logit_ps[:, 0:8], l, rr, start=s, stop=p),
                      r=[lk, rk2], w=[("psF",)], tag=f"{tag}lg{j}")
                k += 1
        em.op("vector", lambda e: e.reduce_max(
            nmx[:, :], logit_ps[:, 0:8], axis=AX.X, negate=True),
            r=[("psF",)], w=[("nmx",)])
        em.op("scalar", lambda e: e.activation(
            out=probs8[:, :], in_=logit_ps[:, 0:8], func=AF.Exp,
            bias=nmx[:, :], accum_out=se_all[:, j:j + 1]),
            r=[("psF",), ("nmx",)], w=[("probs8",), ("se", j)], tag=f"{tag}exp{j}")
        em.op("vector", lambda e: e.reduce_max(
            pv_all[:, 2 * j:2 * j + 1], probs8[:, :], axis=AX.X),
            r=[("probs8",)], w=[("pv", j)])
        em.op("vector", lambda e: e.tensor_scalar(
            msk8[:, :], probs8[:, :], pv_all[:, 2 * j:2 * j + 1], None,
            ALU.is_equal), r=[("probs8",), ("pv", j)], w=[("msk8",)])
        em.op("vector", lambda e: e.select(
            tmp8[:, :], msk8[:, :], iota8[:, :], c8[:, :]),
            r=[("msk8",), ("iota8",), ("c8",)], w=[("tmp8",)])
        em.op("vector", lambda e: e.tensor_reduce(
            idx1[:, :], tmp8[:, :], axis=AX.X, op=ALU.min),
            r=[("tmp8",)], w=[("idx1",)])
        oh1 = oh_all[:, j * 16:j * 16 + 8]
        oh2 = oh_all[:, j * 16 + 8:j * 16 + 16]
        em.op("vector", lambda e: e.tensor_scalar(
            oh1, iota8[:, :], idx1[:, :], None, ALU.is_equal),
            r=[("iota8",), ("idx1",)], w=[("oh", j)])
        em.op("vector", lambda e: e.scalar_tensor_tensor(
            pm8[:, :], oh1, -1e30, probs8[:, :], ALU.mult, ALU.add),
            r=[("oh", j), ("probs8",)], w=[("pm8",)])
        em.op("vector", lambda e: e.reduce_max(
            pv_all[:, 2 * j + 1:2 * j + 2], pm8[:, :], axis=AX.X),
            r=[("pm8",)], w=[("pv", j)])
        em.op("vector", lambda e: e.tensor_scalar(
            msk8[:, :], pm8[:, :], pv_all[:, 2 * j + 1:2 * j + 2], None,
            ALU.is_equal), r=[("pm8",), ("pv", j)], w=[("msk8",)])
        em.op("vector", lambda e: e.select(
            tmp8[:, :], msk8[:, :], iota8[:, :], c8[:, :]),
            r=[("msk8",), ("iota8",), ("c8",)], w=[("tmp8",)])
        em.op("vector", lambda e: e.tensor_reduce(
            idx2[:, :], tmp8[:, :], axis=AX.X, op=ALU.min),
            r=[("tmp8",)], w=[("idx2",)])
        em.op("vector", lambda e: e.tensor_scalar(
            oh2, iota8[:, :], idx2[:, :], None, ALU.is_equal),
            r=[("iota8",), ("idx2",)], w=[("oh", j)])
        if DEBUG_ROUTER:
            em.op("vector", lambda e: e.tensor_copy(
                dbgsb[:, j * 8 + 2:j * 8 + 3], idx1[:, :]),
                r=[("idx1",)], w=[("dbgsb",)])
            em.op("vector", lambda e: e.tensor_copy(
                dbgsb[:, j * 8 + 3:j * 8 + 4], idx2[:, :]),
                r=[("idx2",)], w=[("dbgsb",)])
        em.op("tensor", lambda e: e.matmul(
            psF[0:8, 16 + j:17 + j], oh1, onescol[:, :], start=True, stop=True),
            r=[("oh", j), ("onescol",)], w=[("psF",)])
        em.op("tensor", lambda e: e.matmul(
            psF[0:8, 24 + j:25 + j], oh2, onescol[:, :], start=True, stop=True),
            r=[("oh", j), ("onescol",)], w=[("psF",)])
        em.op("tensor", lambda e: e.matmul(
            psG[:, j * 16:j * 16 + 8], u128[:, :], oh1, start=False, stop=False),
            r=[("u128",), ("oh", j)], w=[("psG",)])
        em.op("tensor", lambda e: e.matmul(
            psG[:, j * 16 + 8:j * 16 + 16], u128[:, :], oh2, start=False,
            stop=False),
            r=[("u128",), ("oh", j)], w=[("psG",)])

    def ROUTER_GLOBAL1(call, tag):
        em.op("vector", lambda e: e.tensor_copy(cnt16[:, :], psF[0:8, 16:32]),
              r=[("psF",)], w=[("cnt16",)])
        em.op("vector", lambda e: e.reduce_sum(
            cntsb[:, 0:1], cnt16[:, 0:8], axis=AX.X),
            r=[("cnt16",)], w=[("cntsb",)])
        em.op("vector", lambda e: e.reduce_sum(
            cntsb[:, 1:2], cnt16[:, 8:16], axis=AX.X),
            r=[("cnt16",)], w=[("cntsb",)])
        em.op("sync", lambda e: e.dma_start(out=cc_in[call].ap(), in_=cntsb[:, :]),
              dma=True, r=[("cntsb",)], w=[("d:ccin", call)])
        for k in range(2):
            em.op("tensor", lambda e, k=k: e.transpose(
                psF[0:8, 40:48], cnt16[:, 8 * k:8 * k + 8], identf[0:8, 0:8]),
                r=[("cnt16",), ("identf",)], w=[("psF",)])
            em.op("vector", lambda e, k=k: e.tensor_copy(
                cntT[:, 8 * k:8 * k + 8], psF[0:8, 40:48]),
                r=[("psF",)], w=[("cntT",)])
            em.op("tensor", lambda e, k=k: e.matmul(
                psF[0:8, 48 + 8 * k:56 + 8 * k], cntT[:, 8 * k:8 * k + 8],
                u128[0:8, 0:8], start=True, stop=True),
                r=[("cntT",), ("u128",)], w=[("psF",)])
        em.op("gpsimd", lambda e: e.collective_compute(
            "AllGather", ALU.bypass, replica_groups=[list(range(8))],
            ins=[cc_in[call].ap().opt()], outs=[cc_out[call].ap().opt()]),
            r=[("d:ccin", call)], w=[("d:ccout", call)], tag=f"{tag}AG")
        em.op("sync", lambda e: e.dma_start(out=gath[:, :], in_=cc_out[call].ap()),
              dma=True, r=[("d:ccout", call)], w=[("gath",)])

    def ROUTER_GLOBAL2(call, tag):
        em.op("tensor", lambda e: e.matmul(
            psF[0:8, 64:65], selm[:, 0:8], gath[:, 0:1], start=True, stop=True),
            r=[("selm",), ("gath",)], w=[("psF",)], tag=f"{tag}base")
        em.op("tensor", lambda e: e.matmul(
            psF[0:8, 65:66], selm[:, 8:16], gath[:, 0:1], start=True, stop=False),
            r=[("selm",), ("gath",)], w=[("psF",)])
        em.op("tensor", lambda e: e.matmul(
            psF[0:8, 65:66], selm[:, 0:8], gath[:, 1:2], start=False, stop=True),
            r=[("selm",), ("gath",)], w=[("psF",)])
        em.op("vector", lambda e: e.tensor_copy(bsb[:, :], psF[0:8, 64:66]),
              r=[("psF",)], w=[("bsb",)])
        for k in range(2):
            em.op("vector", lambda e, k=k: e.tensor_scalar(
                comb[:, 8 * k:8 * k + 8], psF[0:8, 48 + 8 * k:56 + 8 * k],
                bsb[:, k:k + 1], None, ALU.add),
                r=[("psF",), ("bsb",)], w=[("comb",)])
        if DEBUG_ROUTER:
            em.op("vector", lambda e: e.tensor_copy(
                dbgsb2[:, 0:128], psG[:, 0:128]),
                r=[("psG",)], w=[("dbgsb2",)])
            em.op("vector", lambda e: e.tensor_copy(
                dbgsb2[0:8, 128:144], comb[:, :]),
                r=[("comb",)], w=[("dbgsb2",)])
            em.op("vector", lambda e: e.tensor_copy(
                dbgsb2[:, 144:272], oh_all[:, 0:128]),
                r=[("oh", jj) for jj in range(NT)], w=[("dbgsb2",)])
        for j in range(NT):
            for k in range(2):
                em.op("tensor", lambda e, j=j, k=k: e.transpose(
                    psF[0:1, 72:80], comb[:, 8 * k + j:8 * k + j + 1],
                    identf[0:8, 0:8]),
                    r=[("comb",), ("identf",)], w=[("psF",)])
                em.op("vector", lambda e, j=j, k=k: e.tensor_copy(
                    b0row[:, :], psF[0:1, 72:80]),
                    r=[("psF",)], w=[("b0row",)])
                em.op("tensor", lambda e, j=j, k=k: e.matmul(
                    psG[:, j * 16 + 8 * k:j * 16 + 8 * k + 8], ones1[:, :],
                    b0row[:, :], start=False,
                    stop=(j == NT - 1 and k == 1)),
                    r=[("ones1",), ("b0row",)], w=[("psG",)])
        for j in range(NT):
            for k in range(2):
                dst = sc1 if k == 0 else sc2
                dk = ("sc1",) if k == 0 else ("sc2",)
                em.op("vector", lambda e, j=j, k=k, dst=dst:
                      e.scalar_tensor_tensor(
                          tmp8[:, :], psG[:, j * 16 + 8 * k:j * 16 + 8 * k + 8],
                          1.0, oh_all[:, j * 16 + 8 * k:j * 16 + 8 * k + 8],
                          ALU.mult, ALU.mult, accum_out=dst[:, :]),
                      r=[("psG",), ("oh", j)], w=[("tmp8",), dk],
                      tag=f"{tag}pos{j}")
                if DEBUG_ROUTER:
                    em.op("vector", lambda e, j=j, k=k, dst=dst: e.tensor_copy(
                        dbgsb[:, j * 8 + k:j * 8 + k + 1], dst[:, :]),
                        r=[dk], w=[("dbgsb",)])
                em.op("vector", lambda e, j=j, k=k, dst=dst: e.tensor_scalar(
                    dst[:, :], dst[:, :], CAP,
                    pv_all[:, 2 * j + k:2 * j + k + 1], ALU.is_lt, ALU.mult),
                    r=[dk, ("pv", j)], w=[dk])
            em.op("vector", lambda e, j=j: e.tensor_tensor(
                sc3[:, :], sc1[:, :], sc2[:, :], ALU.add),
                r=[("sc1",), ("sc2",)], w=[("sc3",)])
            em.op("vector", lambda e, j=j: e.reciprocal(
                rc1[:, :], se_all[:, j:j + 1]),
                r=[("se", j)], w=[("rc1",)])
            em.op("vector", lambda e, j=j: e.tensor_scalar(
                scol[:, j:j + 1], sc3[:, :], rc1[:, :], 0.5, ALU.mult, ALU.mult),
                r=[("sc3",), ("rc1",)], w=[("scol", j)], tag=f"{tag}s{j}")
            if DEBUG_ROUTER and call == 0:
                em.op("vector", lambda e, j=j: e.tensor_copy(
                    dbgsb[:, j * 8 + 4:j * 8 + 6],
                    pv_all[:, 2 * j:2 * j + 2]),
                    r=[("pv", j)], w=[("dbgsb",)])
                em.op("vector", lambda e, j=j: e.tensor_copy(
                    dbgsb[:, j * 8 + 6:j * 8 + 7], se_all[:, j:j + 1]),
                    r=[("se", j)], w=[("dbgsb",)])
                em.op("vector", lambda e, j=j: e.tensor_copy(
                    dbgsb[:, j * 8 + 7:j * 8 + 8], scol[:, j:j + 1]),
                    r=[("scol", j)], w=[("dbgsb",)])

    xkeys = _gk("xnfm", 0, 16 * XG, XG)

    def W1(tt, tag, call=0):
        for s in range(NS):
            buf = wb[s % 2]
            bk = [("wb", s % 2, 0), ("wb", s % 2, 1)]
            if call == 0:
                em.op("gpsimd", lambda e, s=s, buf=buf: e.dma_start(
                    out=buf[:, :], in_=w1s[s, :, :, :].rearrange("p c f -> p (c f)")),
                    dma=True, w=bk, tag=f"{tag}w1d{s}")
                if tt == 0:
                    em.op("sync", lambda e, s=s, buf=buf: e.dma_start(
                        out=w1bf.ap()[s, :, :], in_=buf[:, :]),
                        dma=True, r=bk, w=[("d:w1bf", s)], tag=f"{tag}w1st{s}")
            else:
                em.op("gpsimd", lambda e, s=s, buf=buf: e.dma_start(
                    out=buf[:, :], in_=w1bf.ap()[s, :, :]),
                    dma=True, r=[("d:w1bf", s)], w=bk, tag=f"{tag}w1d{s}")
            ps = PS[s % 2]
            pk = PSK[s % 2]
            for c in range(16):
                em.op("tensor", lambda e, s=s, c=c, tt=tt, buf=buf, ps=ps:
                      e.matmul(ps[:, :],
                               buf[:, c * 128:(c + 1) * 128],
                               xnfm[:, c * 1024 + tt * 512:c * 1024 + (tt + 1) * 512],
                               start=(c == 0), stop=(c == 15)),
                      r=bk + [xkeys[c]], w=[(pk,)], tag=f"{tag}w1mm")
            em.op("scalar", lambda e, s=s, ps=ps: e.activation(
                out=big[:, s * 512:(s + 1) * 512], in_=ps[:, :],
                func=AF.Gelu, bias=b1sb[:, s:s + 1]),
                r=[(pk,), ("b1sb",)], w=_gk("big", s * 512, (s + 1) * 512, BG),
                tag=f"{tag}gelu{s}")

    def W2(tt, tag, call=0, pre_evac_hook=None):
        for q in range(4):
            banks = [0, 1, 4, 5] if q % 2 == 0 else [6, 0, 1, 4]
            qs = [PS[b] for b in banks]
            qk = [PSK[b] for b in banks]
            for s in range(NS):
                slot = s % 4
                buf = wb[slot % 2]
                half = slot // 2
                bk = ("wb", slot % 2, half)
                dst = buf[:, half * 1024 + 256:half * 1024 + 768] if False else buf[:, half * 1024:half * 1024 + 512]
                if call == 0:
                    em.op("gpsimd", lambda e, q=q, s=s, dst=dst: e.dma_start(
                        out=dst, in_=w2s[q, s * 128:(s + 1) * 128, :]),
                        dma=True, w=[bk], tag=f"{tag}w2d")
                    if tt == 0:
                        em.op("sync", lambda e, q=q, s=s, dst=dst: e.dma_start(
                            out=w2bf.ap()[q, s, :, :], in_=dst),
                            dma=True, r=[bk], w=[("d:w2bf", q, s)],
                            tag=f"{tag}w2st")
                else:
                    em.op("gpsimd", lambda e, q=q, s=s, dst=dst: e.dma_start(
                        out=dst, in_=w2bf.ap()[q, s, :, :]),
                        dma=True, r=[("d:w2bf", q, s)], w=[bk], tag=f"{tag}w2d")
                for u in range(4):
                    em.op("tensor", lambda e, s=s, u=u, dst=dst, ps=qs[u]:
                          e.matmul(ps[:, :],
                                   big[:, s * 512 + u * 128:s * 512 + (u + 1) * 128],
                                   dst[:, 0:512],
                                   start=(s == 0), stop=False),
                          r=[bk] + _gk("big", s * 512, (s + 1) * 512, BG),
                          w=[(qk[u],)], tag=f"{tag}w2mm")
            if q == 0 and pre_evac_hook:
                pre_evac_hook()
            for u in range(4):
                em.op("tensor", lambda e, q=q, u=u, ps=qs[u]: e.matmul(
                    ps[:, :], ones1b[:, :], b2sb[0:1, q * 512:(q + 1) * 512],
                    start=False, stop=True),
                    r=[("ones1b",), ("b2sb",)], w=[(qk[u],)], tag=f"{tag}b2")
            for u in range(4):
                j = tt * 4 + u
                rs = resid[:, j * D + q * 512:j * D + (q + 1) * 512]
                em.op("vector", lambda e, j=j, q=q, u=u, ps=qs[u], rs=rs:
                      e.scalar_tensor_tensor(
                          rs, ps[:, :], scol[:, j:j + 1], rs,
                          ALU.mult, ALU.add,
                          accum_out=accs[:, j * 4 + q:j * 4 + q + 1]),
                      r=[(qk[u],), ("scol", j), ("resid", j)],
                      w=[("resid", j), ("accs", j)], tag=f"{tag}ev{j}q{q}")

    def FFN(call, tag, mid_hook=None, end_hook=None, skip_stats=False):
        sum_from_accs = (call > 0)
        for z in range(4):
            em.op("tensor", lambda e, z=z: e.matmul(
                psG[:, z * 128:(z + 1) * 128], zrow[:, :], identb[0:1, 0:128],
                start=(z == 0), stop=False),
                r=[("zrow",), ("identb",)], w=[("psG",)], tag=f"{tag}psCclr")
        if not skip_stats:
            for j in range(NT):
                LN_STATS(j, sum_from_accs, f"{tag}ln")
            LN_RSTD(list(range(NT)), f"{tag}ln")
        for j in range(4):
            LN_NORM(j, True, f"{tag}ln")
            ROUTER_TILE(j, tag)
        W1(0, tag, call=call)
        for j in range(4, NT):
            LN_NORM(j, True, f"{tag}ln")
            ROUTER_TILE(j, tag)
        cc = 0 if call == 0 else 1
        ROUTER_GLOBAL1(cc, tag)
        ROUTER_GLOBAL2(cc, tag)
        W2(0, tag, call=call)
        if mid_hook:
            mid_hook()
        W1(1, tag, call=call)
        W2(1, tag, call=call)
        if end_hook:
            end_hook()

    def ATTN(tag, want_ln3):
        qfull = _gk("xnfm", 0, 16 * XG, XG)
        ckvfull = _gk("big", 24576, 32768, BG)
        QH0, KH0, VH0 = 16384, 18432, 20480
        qhk = _gk("big", QH0, QH0 + 2048, BG)
        khk = _gk("big", KH0, KH0 + 2048, BG)
        vhk = _gk("big", VH0, VH0 + 2048, BG)

        def QLAT_CKV(th):
            for r in range(8):
                buf = wb[r % 2]
                bk = [("wb", r % 2, 0), ("wb", r % 2, 1)]
                em.op("gpsimd", lambda e, r=r, buf=buf: e.dma_start(
                    out=buf[:, :], in_=wqds[r, :, :, :].rearrange("p c f -> p (c f)")),
                    dma=True, w=bk, tag=f"{tag}qd{r}")
                ps, pk = PS[r % 2], PSK[r % 2]
                for c in range(16):
                    em.op("tensor", lambda e, r=r, th=th, c=c, buf=buf, ps=ps:
                          e.matmul(ps[:, :], buf[:, c * 128:(c + 1) * 128],
                                   xnfm[:, c * 1024 + th * 512:c * 1024 + (th + 1) * 512],
                                   start=(c == 0), stop=(c == 15)),
                          r=bk + [xkeys[c]], w=[(pk,)], tag=f"{tag}qlmm")
                em.op("scalar", lambda e, r=r, th=th, ps=ps: e.activation(
                    out=big[:, r * 1024 + th * 512:r * 1024 + (th + 1) * 512],
                    in_=ps[:, :], func=AF.Copy),
                    r=[(pk,)], w=_gk("big", r * 1024 + th * 512,
                                     r * 1024 + (th + 1) * 512, BG),
                    tag=f"{tag}qlev")
            for r in range(4):
                buf = wb[r % 2]
                bk = [("wb", r % 2, 0), ("wb", r % 2, 1)]
                em.op("gpsimd", lambda e, r=r, buf=buf: e.dma_start(
                    out=buf[:, :], in_=wkvds[r, :, :, :].rearrange("p c f -> p (c f)")),
                    dma=True, w=bk, tag=f"{tag}kvd{r}")
                ps, pk = PS[2 + r % 2], PSK[2 + r % 2]
                for c in range(16):
                    em.op("tensor", lambda e, r=r, th=th, c=c, buf=buf, ps=ps:
                          e.matmul(ps[:, :], buf[:, c * 128:(c + 1) * 128],
                                   xnfm[:, c * 1024 + th * 512:c * 1024 + (th + 1) * 512],
                                   start=(c == 0), stop=(c == 15)),
                          r=bk + [xkeys[c]], w=[(pk,)], tag=f"{tag}ckvmm")
                em.op("scalar", lambda e, r=r, th=th, ps=ps: e.activation(
                    out=big[:, 8192 + r * 1024 + th * 512:8192 + r * 1024 + (th + 1) * 512],
                    in_=ps[:, :], func=AF.Copy),
                    r=[(pk,)], w=_gk("big", 8192 + r * 1024 + th * 512,
                                     8192 + r * 1024 + (th + 1) * 512, BG),
                    tag=f"{tag}ckvev")

        for j in range(4):
            LN_NORM(j, False, f"{tag}ln")
        QLAT_CKV(0)
        for j in range(4, NT):
            LN_NORM(j, False, f"{tag}ln")
        QLAT_CKV(1)
        latvq = lat_in.ap()[0:1024, :].rearrange("(n p) f -> p n f", p=128)
        bigvq = big[:, 0:8 * 1024].rearrange("p (n f) -> p n f", f=1024)
        em.op("sync", lambda e: e.dma_start(out=latvq, in_=bigvq), dma=True,
              r=_gk("big", 0, 8192, BG), w=[("d:latinq",)], tag=f"{tag}latoutq")
        em.op("gpsimd", lambda e: e.collective_compute(
            "AllGather", ALU.bypass,
            replica_groups=[[0, 1], [2, 3], [4, 5], [6, 7]],
            ins=[lat_in.ap()[0:1024, :].opt()],
            outs=[lat_outq.ap().opt()]),
            r=[("d:latinq",)], w=[("d:latoutq",)], tag=f"{tag}AGlatq")
        latvk = lat_in.ap()[1024:1536, :].rearrange("(n p) f -> p n f", p=128)
        bigvk = big[:, 8192:12 * 1024].rearrange("p (n f) -> p n f", f=1024)
        em.op("sync", lambda e: e.dma_start(out=latvk, in_=bigvk), dma=True,
              r=_gk("big", 8192, 12288, BG), w=[("d:latink",)], tag=f"{tag}latoutk")
        em.op("gpsimd", lambda e: e.collective_compute(
            "AllGather", ALU.bypass,
            replica_groups=[[0, 1], [2, 3], [4, 5], [6, 7]],
            ins=[lat_in.ap()[1024:1536, :].opt()],
            outs=[lat_outk.ap().opt()]),
            r=[("d:latink",)], w=[("d:latoutk",)], tag=f"{tag}AGlatk")
        lovq = lat_outq.ap().rearrange("(w n p) f -> w n p f", w=2, p=128)
        lovk = lat_outk.ap().rearrange("(w n p) f -> w n p f", w=2, p=128)
        for r in range(8):
            for who in range(2):
                em.op("sync", lambda e, r=r, who=who: e.dma_start(
                    out=xnfm[:, r * 2048 + who * 1024:r * 2048 + (who + 1) * 1024],
                    in_=lovq[who, r, :, :]),
                    dma=True, r=[("d:latoutq",)],
                    w=_gk("xnfm", r * 2048 + who * 1024, r * 2048 + (who + 1) * 1024, XG),
                    tag=f"{tag}qfull")
        for r in range(4):
            for who in range(2):
                em.op("sync", lambda e, r=r, who=who: e.dma_start(
                    out=big[:, 24576 + r * 2048 + who * 1024:24576 + r * 2048 + (who + 1) * 1024],
                    in_=lovk[who, r, :, :]),
                    dma=True, r=[("d:latoutk",)],
                    w=_gk("big", 24576 + r * 2048 + who * 1024,
                          24576 + r * 2048 + (who + 1) * 1024, BG),
                    tag=f"{tag}ckvfull")

        MX = [mx4, mx4b]
        NMX = [nmx, nmxb]
        SUME = [sume, sumeb]
        SE1 = [se1, se1b]
        RC1 = [rc1, rc1b]
        ABUF = [(hi_t, ("hi_t",)), (lofm, ("lo_t",))]

        def srow(qt):
            klen = (qt + 1) * 128
            off = (qt % 2) * 1024 if qt <= 7 else 0
            keys = [f"psS{b}" for b in range(off // 512,
                                             (off + klen + 511) // 512)]
            return off, klen, keys

        def SCORES(h, qt):
            off, klen, keys = srow(qt)
            for kc in range((klen + 511) // 512):
                w_ = min(512, klen - kc * 512)
                em.op("tensor", lambda e, qt=qt, kc=kc, w_=w_, off=off:
                      e.matmul(psS[:, off + kc * 512:off + kc * 512 + w_],
                               big[:, QH0 + qt * 128:QH0 + (qt + 1) * 128],
                               big[:, KH0 + kc * 512:KH0 + kc * 512 + w_],
                               start=True, stop=True),
                      r=[qhk[(qt * 128) // BG], khk[(kc * 512) // BG]],
                      w=[(keys[kc],)], tag=f"{tag}sc")

        def SOFT(h, qt):
            off, klen, keys = srow(qt)
            kk = [(k,) for k in keys]
            p = qt % 2
            abuf, abk = ABUF[p]
            doff = off + qt * 128
            dk2 = (keys[(qt * 128) // 512],)
            em.op("vector", lambda e, doff=doff: e.tensor_tensor(
                psS[:, doff:doff + 128], psS[:, doff:doff + 128],
                cmask[:, :], ALU.add),
                r=[dk2, ("cmask",)], w=[dk2], tag=f"{tag}mask")
            em.op("vector", lambda e, off=off, klen=klen, p=p: e.reduce_max(
                NMX[p][:, :], psS[:, off:off + klen], axis=AX.X, negate=True),
                r=kk, w=[("nmx", p)], tag=f"{tag}max")
            em.op("scalar", lambda e, off=off, klen=klen, p=p, abuf=abuf:
                  e.activation(
                      out=abuf[:, 0:klen], in_=psS[:, off:off + klen],
                      func=AF.Exp, bias=NMX[p][:, :],
                      accum_out=SE1[p][:, :]),
                  r=kk + [("nmx", p)], w=[abk, ("se1", p)],
                  tag=f"{tag}exp")
            em.op("vector", lambda e, p=p: e.reciprocal(RC1[p][:, :], SE1[p][:, :]),
                  r=[("se1", p)], w=[("rc1", p)])
            em.op("vector", lambda e, p=p, abuf=abuf, klen=klen: e.tensor_scalar(
                abuf[:, 0:klen], abuf[:, 0:klen], RC1[p][:, :], None, ALU.mult),
                r=[abk, ("rc1", p)], w=[abk], tag=f"{tag}anorm")

        def CTX(h, qt):
            p = qt % 2
            abuf, abk = ABUF[p]
            cps, cpk = (psE, "psE") if p == 0 else (psF, "psF")
            for g in range((qt + 1 + 3) // 4):
                n_in_g = min(4, qt + 1 - g * 4)
                for kk in range(n_in_g):
                    kt = g * 4 + kk
                    em.op("tensor", lambda e, kt=kt, kk=kk, abuf=abuf: e.transpose(
                        psT[:, kk * 128:(kk + 1) * 128],
                        abuf[:, kt * 128:(kt + 1) * 128], identb[:, :]),
                        r=[abk, ("identb",)], w=[("psT",)], tag=f"{tag}aT")
                em.op("vector", lambda e, n_in_g=n_in_g: e.tensor_copy(
                    attnT[:, 0:n_in_g * 128], psT[:, 0:n_in_g * 128]),
                    r=[("psT",)], w=[("attnT",)])
                for kk in range(n_in_g):
                    kt = g * 4 + kk
                    em.op("tensor", lambda e, kt=kt, kk=kk, qt=qt, cps=cps:
                          e.matmul(cps[:, 0:128],
                                   big[:, VH0 + kt * 128:VH0 + (kt + 1) * 128],
                                   attnT[:, kk * 128:(kk + 1) * 128],
                                   start=(kt == 0), stop=(kt == qt)),
                          r=[vhk[(kt * 128) // BG], ("attnT",)],
                          w=[(cpk,)], tag=f"{tag}ctxmm")
            em.op("vector", lambda e, h=h, qt=qt, cps=cps: e.tensor_copy(
                big[:, h * 2048 + qt * 128:h * 2048 + (qt + 1) * 128],
                cps[:, 0:128]),
                r=[(cpk,)],
                w=[_gk("big", h * 2048 + qt * 128, h * 2048 + (qt + 1) * 128, BG)[0]],
                tag=f"{tag}ctxev")

        for h in range(NH):
            buf = wb[h % 2]
            bk = [("wb", h % 2, 0), ("wb", h % 2, 1)]
            em.op("gpsimd", lambda e, h=h, buf=buf: e.dma_start(
                out=buf[:, :], in_=wqkv[h, :, :]), dma=True, w=bk,
                tag=f"{tag}whd{h}")
            for tc in range(4):
                ps, pk = PS[tc], PSK[tc]
                for r in range(8):
                    em.op("tensor", lambda e, h=h, tc=tc, r=r, buf=buf, ps=ps:
                          e.matmul(ps[:, :], buf[:, r * 128:(r + 1) * 128],
                                   xnfm[:, r * 2048 + tc * 512:r * 2048 + (tc + 1) * 512],
                                   start=(r == 0), stop=(r == 7)),
                          r=bk + [qfull[(r * 2048 + tc * 512) // XG]],
                          w=[(pk,)], tag=f"{tag}qhmm")
                em.op("scalar", lambda e, tc=tc, ps=ps: e.activation(
                    out=big[:, QH0 + tc * 512:QH0 + (tc + 1) * 512], in_=ps[:, :],
                    func=AF.Copy, scale=c_qs[:, :]),
                    r=[(pk,), ("c_qs",)], w=[qhk[(tc * 512) // BG]],
                    tag=f"{tag}qhev")
            for tc in range(4):
                ps, pk = PS[tc], PSK[tc]
                for r in range(4):
                    em.op("tensor", lambda e, h=h, tc=tc, r=r, buf=buf, ps=ps:
                          e.matmul(ps[:, :], buf[:, 1024 + r * 128:1024 + (r + 1) * 128],
                                   big[:, 24576 + r * 2048 + tc * 512:24576 + r * 2048 + (tc + 1) * 512],
                                   start=(r == 0), stop=(r == 3)),
                          r=bk + [ckvfull[(r * 2048 + tc * 512) // BG]],
                          w=[(pk,)], tag=f"{tag}khmm")
                em.op("scalar", lambda e, tc=tc, ps=ps: e.activation(
                    out=big[:, KH0 + tc * 512:KH0 + (tc + 1) * 512], in_=ps[:, :],
                    func=AF.Copy),
                    r=[(pk,)], w=[khk[(tc * 512) // BG]], tag=f"{tag}khev")
            for kt in range(16):
                ps, pk = (psE, "psE") if kt % 2 == 0 else (psF, "psF")
                for r in range(4):
                    em.op("tensor", lambda e, kt=kt, r=r, buf=buf, ps=ps:
                          e.matmul(ps[:, 0:128],
                                   big[:, 24576 + r * 2048 + kt * 128:24576 + r * 2048 + (kt + 1) * 128],
                                   buf[:, 1536 + r * 128:1536 + (r + 1) * 128],
                                   start=(r == 0), stop=(r == 3)),
                          r=bk + [ckvfull[(r * 2048 + kt * 128) // BG]],
                          w=[(pk,)], tag=f"{tag}vhmm")
                em.op("vector", lambda e, kt=kt, ps=ps: e.tensor_copy(
                    big[:, VH0 + kt * 128:VH0 + (kt + 1) * 128], ps[:, 0:128]),
                    r=[(pk,)], w=[vhk[(kt * 128) // BG]], tag=f"{tag}vhev")
            SCORES(h, 0)
            for qt in range(16):
                if qt < 15 and qt + 1 <= 7:
                    SCORES(h, qt + 1)
                    SOFT(h, qt)
                else:
                    SOFT(h, qt)
                    if qt < 15:
                        SCORES(h, qt + 1)
                CTX(h, qt)
            for d in range(2):
                civ = ctx_in.ap()[d * 1024 + h * 128:d * 1024 + (h + 1) * 128, :]
                em.op("sync", lambda e, h=h, d=d, civ=civ: e.dma_start(
                    out=civ,
                    in_=big[:, h * 2048 + d * 1024:h * 2048 + (d + 1) * 1024]),
                    dma=True,
                    r=_gk("big", h * 2048 + d * 1024, h * 2048 + (d + 1) * 1024, BG),
                    w=[("d:ctxin", h, d)], tag=f"{tag}ctxout{h}")
        em.op("gpsimd", lambda e: e.collective_compute(
            "AllGather", ALU.bypass,
            replica_groups=[list(range(8))],
            ins=[ctx_in.ap().opt()], outs=[ctx_out.ap().opt()]),
            r=[("d:ctxin", h2, d2) for h2 in range(NH) for d2 in range(2)],
            w=[("d:ctxout",)], tag=f"{tag}AGctx")
        wosk = _gk("big", 0, 16384, BG)
        for c in range(16):
            em.op("gpsimd", lambda e, c=c: e.dma_start(
                out=big[:, c * 1024:(c + 1) * 1024],
                in_=wos[c, :, 0:1024]),
                dma=True, w=wosk[c:c + 1], tag=f"{tag}wod0")
        for c in range(16):
            em.op("gpsimd", lambda e, c=c: e.indirect_dma_start(
                out=big[:, 16384 + c * 1024:16384 + (c + 1) * 1024],
                out_offset=None,
                in_=ctx_out.ap()[0:128, :],
                in_offset=bass.IndirectOffsetOnAxis(ap=ctxidx[:, c:c + 1], axis=0),
                bounds_check=16384),
                dma=True, r=[("d:ctxout",), ("ctxidx",)],
                w=_gk("big", 16384 + c * 1024, 16384 + (c + 1) * 1024, BG),
                tag=f"{tag}ctxall")
        ctxall = _gk("big", 16384, 32768, BG)
        for dh in range(2):
            if dh == 1:
                for c in range(16):
                    em.op("gpsimd", lambda e, c=c: e.dma_start(
                        out=big[:, c * 1024:(c + 1) * 1024],
                        in_=wos[c, :, 1024:2048]),
                        dma=True, w=wosk[c:c + 1], tag=f"{tag}wod")
            for jg in range(4):
                banks = [0, 1, 2, 3] if jg % 2 == 0 else [4, 5, 6, 2]
                for c in range(16):
                    for u in range(2):
                        j = jg * 2 + u
                        for i in range(2):
                            ps, pk = PS[banks[u * 2 + i]], PSK[banks[u * 2 + i]]
                            em.op("tensor", lambda e, c=c, j=j, i=i, ps=ps:
                                  e.matmul(
                                      ps[:, 0:512],
                                      big[:, 16384 + c * 1024 + j * 128:16384 + c * 1024 + (j + 1) * 128],
                                      big[:, c * 1024 + i * 512:c * 1024 + (i + 1) * 512],
                                      start=(c == 0), stop=False),
                                  r=[wosk[c], ctxall[(c * 1024 + j * 128) // BG]],
                                  w=[(pk,)], tag=f"{tag}womm")
                for u in range(2):
                    j = jg * 2 + u
                    for i in range(2):
                        dc = dh * 2 + i
                        ps, pk = PS[banks[u * 2 + i]], PSK[banks[u * 2 + i]]
                        em.op("tensor", lambda e, dc=dc, ps=ps: e.matmul(
                            ps[:, 0:512], ones1b[:, :],
                            bosb[0:1, dc * 512:(dc + 1) * 512],
                            start=False, stop=True),
                            r=[("ones1b",), ("bosb",)], w=[(pk,)], tag=f"{tag}bo")
                        rs = resid[:, j * D + dc * 512:j * D + (dc + 1) * 512]
                        em.op("vector", lambda e, j=j, dc=dc, ps=ps, rs=rs:
                              e.scalar_tensor_tensor(
                                  rs, ps[:, 0:512], 1.0, rs, ALU.mult, ALU.add,
                                  accum_out=accs[:, j * 4 + dc:j * 4 + dc + 1]),
                              r=[(pk,), ("resid", j)],
                              w=[("resid", j), ("accs", j)], tag=f"{tag}woev{j}")
                    if dh == 1 and want_ln3:
                        LN_STATS(j, True, "F3ln")
        if want_ln3:
            LN_RSTD(list(range(NT)), "F3ln")

    def P_OUT():
        if DEBUG_ROUTER:
            em.op("sync", lambda e: e.dma_start(out=dbg, in_=dbgsb[:, :]),
                  dma=True, r=[("dbgsb",)], w=[("d:dbg",)])
            em.op("sync", lambda e: e.dma_start(out=dbg2, in_=dbgsb2[:, :]),
                  dma=True, r=[("dbgsb2",)], w=[("d:dbg2",)])
        for j in range(NT):
            em.op("sync", lambda e, j=j: e.dma_start(
                out=yout[j * 128:(j + 1) * 128, :], in_=resid[:, j * D:(j + 1) * D]),
                dma=True, r=[("resid", j)], w=[("d:y", j)], tag=f"out{j}")

    P0()

    def ln2_mid():
        for j in range(4):
            LN_STATS(j, True, "A2ln")

    def ln2_end():
        for j in range(4, NT):
            LN_STATS(j, True, "A2ln")
        LN_RSTD(list(range(NT)), "A2ln")

    if n_sub >= 2:
        FFN(0, "F1", mid_hook=ln2_mid, end_hook=ln2_end)
        ATTN("A", want_ln3=(n_sub >= 3))
        if n_sub >= 3:
            FFN(1, "F3", skip_stats=True)
    else:
        FFN(0, "F1")
    P_OUT()
    em.finalize(nc)
    stack.close()
    return nc


def _host_layouts(L0_attn_Wq_down, L0_attn_Wkv_down, L0_attn_Wq_up,
                  L0_attn_Wk_up, L0_attn_Wv_up, L0_attn_Wo, L0_attn_bo,
                  L0_ffn_router_w, L0_ffn_w1, L0_ffn_b1, L0_ffn_w2, L0_ffn_b2):
    c = np.ascontiguousarray
    w1T = c(np.asarray(L0_ffn_w1).T)
    w1s = c(w1T.reshape(16, 128, 64, 128).transpose(2, 1, 0, 3))
    w2s = c(np.asarray(L0_ffn_w2).T.reshape(8192, 4, 512).transpose(1, 0, 2))
    b1s = c(np.asarray(L0_ffn_b1).reshape(64, 128).T)
    b2row = c(np.asarray(L0_ffn_b2)[None, :])
    rT = c(np.asarray(L0_ffn_router_w).T.reshape(16, 128, 8).transpose(1, 0, 2)
           .reshape(128, 128))
    wqds = c(np.asarray(L0_attn_Wq_down).T.reshape(16, 128, 8, 128)
             .transpose(2, 1, 0, 3))
    wkvds = c(np.asarray(L0_attn_Wkv_down).T.reshape(16, 128, 4, 128)
              .transpose(2, 1, 0, 3))
    wquT = np.asarray(L0_attn_Wq_up).T.reshape(8, 128, 16, 128).transpose(2, 0, 1, 3)
    wkuT = np.asarray(L0_attn_Wk_up).T.reshape(4, 128, 16, 128).transpose(2, 0, 1, 3)
    wvuT = np.asarray(L0_attn_Wv_up).T.reshape(4, 128, 16, 128).transpose(2, 0, 1, 3)
    wqkv = np.concatenate([
        wquT.transpose(0, 2, 1, 3).reshape(16, 128, 1024),
        wkuT.transpose(0, 2, 1, 3).reshape(16, 128, 512),
        wvuT.transpose(0, 2, 1, 3).reshape(16, 128, 512)], axis=2)
    wqkv = c(wqkv)
    wos = c(np.asarray(L0_attn_Wo).T.reshape(16, 128, 2048))
    borow = c(np.asarray(L0_attn_bo)[None, :])
    return dict(w1s=w1s, w2s=w2s, b1s=b1s, b2row=b2row, rT=rT, wqds=wqds,
                wkvds=wkvds, wqkv=wqkv, wos=wos, borow=borow)


_cache = {}


def kernel(L0_attn_x, L0_attn_Wq_down, L0_attn_Wkv_down, L0_attn_Wq_up,
           L0_attn_Wk_up, L0_attn_Wv_up, L0_attn_Wo, L0_attn_bo,
           L0_ffn_router_w, L0_ffn_w1, L0_ffn_b1, L0_ffn_w2, L0_ffn_b2):
    x = np.ascontiguousarray(np.asarray(L0_attn_x, dtype=np.float32))
    shared = _host_layouts(L0_attn_Wq_down, L0_attn_Wkv_down, L0_attn_Wq_up,
                           L0_attn_Wk_up, L0_attn_Wv_up, L0_attn_Wo,
                           L0_attn_bo, L0_ffn_router_w, L0_ffn_w1, L0_ffn_b1,
                           L0_ffn_w2, L0_ffn_b2)
    in_maps = []
    for cid in range(8):
        b, half = cid // 2, cid % 2
        m = dict(shared)
        m["x"] = np.ascontiguousarray(x[b, half * T:(half + 1) * T, :])
        sel = np.zeros((64, 16), np.float32)
        for cp in range(8):
            for e in range(8):
                if cp < cid:
                    sel[cp * 8 + e, e] = 1.0
                sel[cp * 8 + e, 8 + e] = 1.0
        m["selmats"] = sel
        m["wqkv"] = np.ascontiguousarray(shared["wqkv"][half * 8:(half + 1) * 8])
        cix = np.empty((128, 16), np.int32)
        for r in range(2):
            rank = 2 * b + r
            for hh in range(8):
                cix[:, r * 8 + hh] = (rank * 2048 + half * 1024 + hh * 128
                                      + np.arange(128))
        m["ctxidx_h"] = cix
        in_maps.append(m)
    key = N_SUBLAYERS
    if key not in _cache:
        _cache[key] = build_program(N_SUBLAYERS)
    nc = _cache[key]
    res = run_bass_kernel_spmd(nc, in_maps, core_ids=list(range(8)))
    out = np.empty((4, 2048, 2048), np.float32)
    for cid in range(8):
        b, half = cid // 2, cid % 2
        out[b, half * T:(half + 1) * T, :] = res.results[cid]["y"]
    return out


# version 06
# speedup vs baseline: 1.1589x; 1.1589x over previous
import numpy as np

import concourse.bass as bass
import concourse.mybir as mybir
from concourse.bass_utils import run_bass_kernel_spmd
from concourse.aot_env import allow_ici_all_to_all

F32 = mybir.dt.float32
BF16 = mybir.dt.bfloat16
I32 = mybir.dt.int32
U8 = mybir.dt.uint8
AF = mybir.ActivationFunctionType
ALU = mybir.AluOpType
AX = mybir.AxisListType

D = 2048
T = 1024
NT = 8
F = 8192
NS = 64
E = 8
CAP = 512.0
RQ = 1024
RK = 512
NH = 8
EPS = 1e-5
QSCALE = 1.0 / np.sqrt(128.0)

N_SUBLAYERS = 3
DEBUG_ROUTER = False

ENGINES = ("tensor", "vector", "scalar", "gpsimd", "sync")
SAME_ENG_SYNC = {"vector": True, "scalar": True, "gpsimd": True,
                 "tensor": False, "sync": False}
DMA_LANES = {"sync": 6, "gpsimd": 6, "scalar": 2, "vector": 1, "tensor": 1}


class _Op:
    __slots__ = ("eng", "fn", "dma", "deps", "needed", "seq", "waits",
                 "vc_after", "tag", "lane")

    def __init__(self, eng, fn, dma, deps, tag):
        self.eng = eng
        self.fn = fn
        self.dma = dma
        self.deps = deps
        self.needed = False
        self.seq = None
        self.waits = ()
        self.vc_after = None
        self.tag = tag
        self.lane = None


class EM:

    def __init__(self):
        self.plan = []
        self.last_writer = {}
        self.readers = {}
        self.lane_rr = {e: 0 for e in ENGINES}
        self.lane_last = {}

    def op(self, eng, fn, r=(), w=(), dma=False, tag=""):
        deps = {}
        for k in r:
            d = self.last_writer.get(k)
            if d is not None:
                deps[id(d)] = d
        for k in w:
            d = self.last_writer.get(k)
            if d is not None:
                deps[id(d)] = d
            for rd in self.readers.get(k, ()):
                deps[id(rd)] = rd
        o = _Op(eng, fn, dma, list(deps.values()), tag)
        if dma:
            lane = self.lane_rr[eng] % DMA_LANES[eng]
            self.lane_rr[eng] += 1
            o.lane = lane
            prev = self.lane_last.get((eng, lane))
            if prev is not None:
                deps.setdefault(id(prev), prev)
                o.deps = list(deps.values())
            self.lane_last[(eng, lane)] = o
        for k in r:
            self.readers.setdefault(k, []).append(o)
        for k in w:
            self.last_writer[k] = o
            self.readers[k] = []
        self.plan.append(o)
        return o

    def finalize(self, nc):
        for o in self.plan:
            for d in o.deps:
                if d.dma:
                    continue
                if d.eng == o.eng and not SAME_ENG_SYNC[o.eng]:
                    continue
                d.needed = True
        ccount = {e: 0 for e in ENGINES}
        dcount = {}
        for o in self.plan:
            if o.dma:
                lk = (o.eng, o.lane)
                dcount[lk] = dcount.get(lk, 0) + 1
                o.seq = 16 * dcount[lk]
                o.needed = True
            elif o.needed:
                ccount[o.eng] += 1
                o.seq = ccount[o.eng]
        vclock = {e: {} for e in ENGINES}
        for o in self.plan:
            vc = vclock[o.eng]
            waits = {}
            for d in o.deps:
                same = (d.eng == o.eng) and not d.dma
                if d.vc_after:
                    for s2, v2 in d.vc_after.items():
                        if vc.get(s2, 0) < v2:
                            vc[s2] = v2
                if same and not SAME_ENG_SYNC[o.eng]:
                    continue
                stream = ("d", d.eng, d.lane) if d.dma else ("c", d.eng)
                v = d.seq
                assert v is not None, f"dep without seq: {d.tag} <- {o.tag}"
                if vc.get(stream, 0) < v and waits.get(stream, 0) < v:
                    waits[stream] = v
            for stream, v in waits.items():
                vc[stream] = max(vc.get(stream, 0), v)
            o.waits = sorted(waits.items())
            o.vc_after = dict(vc)
        self.final_dma = {(e, l): 16 * n for (e, l), n in dcount.items() if n}
        self.final_comp = {e: n for e, n in ccount.items() if n}

        sems = {}
        ctxs = []
        for e in ENGINES:
            s = nc.semaphore(f"c_{e}")
            sems[("c", e)] = s.__enter__()
            ctxs.append(s)
            for l in range(DMA_LANES[e]):
                if any((e2, l2) == (e, l) for (e2, l2) in dcount) or True:
                    s = nc.semaphore(f"d_{e}{l}")
                    sems[("d", e, l)] = s.__enter__()
                    ctxs.append(s)
        blk = nc.Block()
        block = blk.__enter__()
        per_eng = {e: [] for e in ENGINES}
        for o in self.plan:
            per_eng[o.eng].append(o)

        def mk(eng_name):
            ops = per_eng[eng_name]

            def body(eng):
                for o in ops:
                    for stream, v in o.waits:
                        eng.wait_ge(sems[stream], v)
                    ins = o.fn(eng)
                    if o.tag:
                        ins.annotate(o.tag)
                    if o.needed:
                        if o.dma:
                            ins.then_inc(sems[("d", o.eng, o.lane)], 16)
                        else:
                            ins.then_inc(sems[("c", o.eng)], 1)
                if eng_name == "sync":
                    for (e2, l2), v in self.final_dma.items():
                        eng.wait_ge(sems[("d", e2, l2)], v)
                    for e2, v in self.final_comp.items():
                        if e2 != "sync":
                            eng.wait_ge(sems[("c", e2)], v)
            return body

        block.tensor(mk("tensor"))
        block.vector(mk("vector"))
        block.scalar(mk("scalar"))
        block.gpsimd(mk("gpsimd"))
        block.sync(mk("sync"))
        blk.__exit__(None, None, None)
        for s in ctxs:
            s.__exit__(None, None, None)


def _gk(name, lo, hi, gran):
    return [(name, g) for g in range(lo // gran, (hi + gran - 1) // gran)]


def build_program(n_sub=3):
    nc = bass.Bass("TRN2", target_bir_lowering=False)
    em = EM()

    xin = nc.dram_tensor("x", [T, D], F32, kind="ExternalInput").ap()
    w1s = nc.dram_tensor("w1s", [NS, 128, 16, 128], F32, kind="ExternalInput").ap()
    w2s = nc.dram_tensor("w2s", [4, F, 512], F32, kind="ExternalInput").ap()
    b1s = nc.dram_tensor("b1s", [128, 64], F32, kind="ExternalInput").ap()
    b2row = nc.dram_tensor("b2row", [1, D], F32, kind="ExternalInput").ap()
    rTin = nc.dram_tensor("rT", [128, 128], F32, kind="ExternalInput").ap()
    selin = nc.dram_tensor("selmats", [64, 16], F32, kind="ExternalInput").ap()
    wqds = nc.dram_tensor("wqds", [8, 128, 16, 128], F32, kind="ExternalInput").ap()
    wkvds = nc.dram_tensor("wkvds", [4, 128, 16, 128], F32, kind="ExternalInput").ap()
    wqkv = nc.dram_tensor("wqkv", [NH, 128, 2048], F32, kind="ExternalInput").ap()
    wos = nc.dram_tensor("wos", [16, 128, 2048], F32, kind="ExternalInput").ap()
    borow = nc.dram_tensor("borow", [1, D], F32, kind="ExternalInput").ap()
    ctxidx_in = nc.dram_tensor("ctxidx_h", [128, 16], I32, kind="ExternalInput").ap()
    yout = nc.dram_tensor("y", [T, D], F32, kind="ExternalOutput").ap()
    dbg = nc.dram_tensor("dbg", [128, NT * 8], F32, kind="ExternalOutput").ap() \
        if DEBUG_ROUTER else None
    dbg2 = nc.dram_tensor("dbg2", [128, 272], F32, kind="ExternalOutput").ap() \
        if DEBUG_ROUTER else None
    dbg3 = nc.dram_tensor("dbg3", [128, 24576], BF16, kind="ExternalOutput").ap() \
        if DEBUG_ROUTER else None

    cc_in = [nc.dram_tensor(f"cc_in{i}", [E, 2], F32) for i in range(2)]
    cc_out = [nc.dram_tensor(f"cc_out{i}", [8 * E, 2], F32) for i in range(2)]
    lat_in = nc.dram_tensor("lat_in", [1536, 1024], BF16)
    lat_outq = nc.dram_tensor("lat_outq", [2048, 1024], BF16)
    lat_outk = nc.dram_tensor("lat_outk", [1024, 1024], BF16)
    ctx_in = nc.dram_tensor("ctx_in", [2048, 1024], BF16)
    ctx_outA = nc.dram_tensor("ctx_outA", [2048, 1024], BF16)
    ctx_outB = nc.dram_tensor("ctx_outB", [2048, 1024], BF16)
    CORDER = [0, 1, 2, 3, 8, 9, 10, 11, 4, 5, 6, 7, 12, 13, 14, 15]
    w1bf = nc.dram_tensor("w1bf", [NS, 128, 2048], BF16)
    w2bf = nc.dram_tensor("w2bf", [4, NS, 128, 512], BF16)

    import contextlib
    stack = contextlib.ExitStack()

    def sb(name, shape, dtype):
        return stack.enter_context(nc.sbuf_tensor(name, shape, dtype))

    def psum(name, shape, dtype=F32):
        return stack.enter_context(nc.psum_tensor(name, shape, dtype))

    resid = sb("resid", [128, NT * D], F32)
    xnfm = sb("xnfm", [128, 16 * 1024], BF16)
    big = sb("big", [128, 64 * 512], BF16)
    xn32 = sb("xn32", [128, D], F32)
    hi_t = sb("hi_t", [128, D], BF16)
    lofm = sb("lofm", [128, D], BF16)
    wb = [sb("wb0", [128, 2048], BF16), sb("wb1", [128, 2048], BF16)]
    rt32 = sb("rt32", [128, 128], F32)
    rthi = sb("rthi", [128, 128], BF16)
    rtlo = sb("rtlo", [128, 128], BF16)
    b1sb = sb("b1sb", [128, 64], F32)
    b2sb = sb("b2sb", [1, D], BF16)
    bosb = sb("bosb", [1, D], BF16)
    ones1b = sb("ones1b", [1, 128], BF16)
    selm = sb("selm", [64, 16], F32)
    gath = sb("gath", [64, 2], F32)
    cntsb = sb("cntsb", [E, 2], F32)
    cnt16 = sb("cnt16", [E, 16], F32)
    cntT = sb("cntT", [8, 16], F32)
    comb = sb("comb", [E, 16], F32)
    bsb = sb("bsb", [E, 2], F32)
    b0row = sb("b0row", [1, E], F32)
    b1row = sb("b1row", [1, E], F32)
    identf = sb("identf", [128, 128], F32)
    identb = sb("identb", [128, 128], BF16)
    u128 = sb("u128", [128, 128], F32)
    cmask = sb("cmask", [128, 128], F32)
    ioti = sb("ioti", [128, 8], I32)
    ctxidx = sb("ctxidx", [128, 16], I32)
    iota8 = sb("iota8", [128, 8], F32)
    c8 = sb("c8", [128, 8], F32)
    ones1 = sb("ones1", [1, 128], F32)
    zrow = sb("zrow", [1, 128], BF16)
    onescol = sb("onescol", [128, 1], F32)
    c_eps = sb("c_eps", [128, 1], F32)
    c_mhalf = sb("c_mhalf", [128, 1], F32)
    c_qs = sb("c_qs", [128, 1], F32)
    sums = sb("sums", [128, NT], F32)
    sumsq = sb("sumsq", [128, NT], F32)
    mcol = sb("mcol", [128, NT], F32)
    rstd = sb("rstd", [128, NT], F32)
    vcol = sb("vcol", [128, NT], F32)
    nmr = sb("nmr", [128, NT], F32)
    accs = sb("accs", [128, NT * 4], F32)
    probs8 = sb("probs8", [128, 8], F32)
    pm8 = sb("pm8", [128, 8], F32)
    tmp8 = sb("tmp8", [128, 8], F32)
    msk8 = sb("msk8", [128, 8], U8)
    oh_all = sb("oh_all", [128, NT * 16], F32)
    pv_all = sb("pv_all", [128, NT * 2], F32)
    se_all = sb("se_all", [128, NT], F32)
    scol = sb("scol", [128, NT], F32)
    sc1 = sb("sc1", [128, 1], F32)
    sc2 = sb("sc2", [128, 1], F32)
    sc3 = sb("sc3", [128, 1], F32)
    dbgsb = sb("dbgsb", [128, NT * 8], F32) if DEBUG_ROUTER else None
    dbgsb2 = sb("dbgsb2", [128, 272], F32) if DEBUG_ROUTER else None
    dbgcols = sb("dbgcols", [128, 16], F32) if DEBUG_ROUTER else None
    dbgctx = sb("dbgctx", [128, 128], F32) if DEBUG_ROUTER else None
    idx1 = sb("idx1", [128, 1], F32)
    idx2 = sb("idx2", [128, 1], F32)
    mx4 = sb("mx4", [128, 4], F32)
    mx4b = sb("mx4b", [128, 4], F32)
    nmx = sb("nmx", [128, 1], F32)
    nmxb = sb("nmxb", [128, 1], F32)
    sume = sb("sume", [128, 4], F32)
    sumeb = sb("sumeb", [128, 4], F32)
    se1 = sb("se1", [128, 1], F32)
    se1b = sb("se1b", [128, 1], F32)
    rc1 = sb("rc1", [128, 1], F32)
    rc1b = sb("rc1b", [128, 1], F32)
    rbcb = sb("rbcb", [128, 128], F32)
    rrow = sb("rrow", [1, 128], F32)
    rbc = sb("rbc", [128, 128], F32)
    attnT = sb("attnT", [128, 1024], BF16)

    psS = psum("psS", [128, 2048])
    psE = psum("psE", [128, 512])
    psF = psum("psF", [128, 512])
    psG = psum("psG", [128, 512])
    psT = psum("psT", [128, 1024], BF16)
    psS0 = psS[:, 0:512]
    psS1 = psS[:, 512:1024]
    psS2_ = psS[:, 1024:1536]
    psS3 = psS[:, 1536:2048]
    PS = [psS0, psS1, psS2_, psS3, psE, psF, psG]
    PSK = ["psS0", "psS1", "psS2", "psS3", "psE", "psF", "psG"]

    RG = 2048
    XG = 1024
    BG = 1024
    LOK = [("lo_t",)]
    HIK = [("hi_t",)]

    def P0():
        em.op("gpsimd", lambda e: e.memset(identf[:, :], 0.0), w=[("identf",)])
        em.op("gpsimd", lambda e: e.affine_select(
            out=identf[:, :], in_=identf[:, :], compare_op=ALU.not_equal,
            fill=1.0, base=0, pattern=[[-1, 128]], channel_multiplier=1),
            r=[("identf",)], w=[("identf",)])
        em.op("gpsimd", lambda e: e.memset(identb[:, :], 0.0), w=[("identb",)])
        em.op("gpsimd", lambda e: e.affine_select(
            out=identb[:, :], in_=identb[:, :], compare_op=ALU.not_equal,
            fill=1.0, base=0, pattern=[[-1, 128]], channel_multiplier=1),
            r=[("identb",)], w=[("identb",)])
        em.op("gpsimd", lambda e: e.memset(u128[:, :], 0.0), w=[("u128",)])
        em.op("gpsimd", lambda e: e.affine_select(
            out=u128[:, :], in_=u128[:, :], compare_op=ALU.is_ge,
            fill=1.0, base=0, pattern=[[-1, 128]], channel_multiplier=1),
            r=[("u128",)], w=[("u128",)])
        em.op("gpsimd", lambda e: e.memset(cmask[:, :], 0.0), w=[("cmask",)])
        em.op("gpsimd", lambda e: e.affine_select(
            out=cmask[:, :], in_=cmask[:, :], compare_op=ALU.is_ge,
            fill=-1e30, base=0, pattern=[[-1, 128]], channel_multiplier=1),
            r=[("cmask",)], w=[("cmask",)])
        em.op("gpsimd", lambda e: e.iota(ioti[:, :], pattern=[[1, 8]], base=0,
                                         channel_multiplier=0), w=[("ioti",)])
        em.op("vector", lambda e: e.tensor_copy(iota8[:, :], ioti[:, :]),
              r=[("ioti",)], w=[("iota8",)])
        em.op("gpsimd", lambda e: e.memset(c8[:, :], 8.0), w=[("c8",)])
        em.op("gpsimd", lambda e: e.memset(ones1[:, :], 1.0), w=[("ones1",)])
        em.op("gpsimd", lambda e: e.memset(zrow[:, :], 0.0), w=[("zrow",)])
        em.op("gpsimd", lambda e: e.memset(ones1b[:, :], 1.0), w=[("ones1b",)])
        em.op("gpsimd", lambda e: e.memset(onescol[:, :], 1.0), w=[("onescol",)])
        em.op("gpsimd", lambda e: e.memset(c_eps[:, :], EPS), w=[("c_eps",)])
        em.op("gpsimd", lambda e: e.memset(c_mhalf[:, :], -0.5), w=[("c_mhalf",)])
        em.op("gpsimd", lambda e: e.memset(c_qs[:, :], QSCALE), w=[("c_qs",)])
        em.op("sync", lambda e: e.dma_start(out=b1sb[:, :], in_=b1s[:, :]),
              dma=True, w=[("b1sb",)])
        em.op("gpsimd", lambda e: e.dma_start(out=b2sb[:, :], in_=b2row[:, :]),
              dma=True, w=[("b2sb",)])
        em.op("gpsimd", lambda e: e.dma_start(out=bosb[:, :], in_=borow[:, :]),
              dma=True, w=[("bosb",)])
        em.op("sync", lambda e: e.dma_start(out=selm[:, :], in_=selin[:, :]),
              dma=True, w=[("selm",)])
        em.op("sync", lambda e: e.dma_start(out=rt32[:, :], in_=rTin[:, :]),
              dma=True, w=[("rt32",)])
        em.op("sync", lambda e: e.dma_start(out=ctxidx[:, :], in_=ctxidx_in[:, :]),
              dma=True, w=[("ctxidx",)])
        em.op("vector", lambda e: e.tensor_copy(rthi[:, :], rt32[:, :]),
              r=[("rt32",)], w=[("rthi",)])
        em.op("vector", lambda e: e.tensor_tensor(rtlo[:, :], rt32[:, :],
                                                  rthi[:, :], ALU.subtract),
              r=[("rt32",), ("rthi",)], w=[("rtlo",)])
        for j in range(NT):
            em.op("sync", lambda e, j=j: e.dma_start(
                out=resid[:, j * D:(j + 1) * D], in_=xin[j * 128:(j + 1) * 128, :]),
                dma=True, w=[("resid", j)])
            em.op("vector", lambda e, j=j: e.reduce_sum(
                sums[:, j:j + 1], resid[:, j * D:(j + 1) * D], axis=AX.X),
                r=[("resid", j)], w=[("sums", j)])

    def LN_STATS(j, sum_from_accs, tag):
        rk = ("resid", j)
        if sum_from_accs:
            em.op("vector", lambda e: e.reduce_sum(
                sums[:, j:j + 1], accs[:, j * 4:(j + 1) * 4], axis=AX.X),
                r=[("accs", j)], w=[("sums", j)], tag=f"{tag}sum{j}")
        em.op("scalar", lambda e: e.activation(
            out=xn32[:, :], in_=resid[:, j * D:(j + 1) * D], func=AF.Square,
            accum_out=sumsq[:, j:j + 1]),
            r=[rk], w=[("xn32",), ("sumsq", j)], tag=f"{tag}sq{j}")
        em.op("vector", lambda e: e.tensor_scalar(
            mcol[:, j:j + 1], sums[:, j:j + 1], 1.0 / D, None, ALU.mult),
            r=[("sums", j)], w=[("mcol", j)])
        em.op("vector", lambda e: e.tensor_scalar(
            vcol[:, j:j + 1], sumsq[:, j:j + 1], 1.0 / D, None, ALU.mult),
            r=[("sumsq", j)], w=[("vcol", j)])
        em.op("vector", lambda e: e.tensor_scalar(
            sc1[:, :], mcol[:, j:j + 1], mcol[:, j:j + 1], None, ALU.mult),
            r=[("mcol", j)], w=[("sc1",)])
        em.op("vector", lambda e: e.tensor_tensor(
            vcol[:, j:j + 1], vcol[:, j:j + 1], sc1[:, :], ALU.subtract),
            r=[("vcol", j), ("sc1",)], w=[("vcol", j)], tag=f"{tag}var{j}")

    def LN_RSTD(js, tag):
        j0, j1 = js[0], js[-1] + 1
        rr = [("vcol", j) for j in js]
        em.op("scalar", lambda e: e.activation(
            out=rstd[:, j0:j1], in_=vcol[:, j0:j1], func=AF.Ln,
            bias=c_eps[:, :]),
            r=rr + [("c_eps",)], w=[("rstd_raw", j0)], tag=f"{tag}ln")
        em.op("scalar", lambda e: e.activation(
            out=rstd[:, j0:j1], in_=rstd[:, j0:j1], func=AF.Exp,
            scale=c_mhalf[:, :]),
            r=[("rstd_raw", j0), ("c_mhalf",)],
            w=[("rstd", j) for j in js], tag=f"{tag}exp")
        for j in js:
            em.op("vector", lambda e, j=j: e.tensor_scalar(
                nmr[:, j:j + 1], mcol[:, j:j + 1], rstd[:, j:j + 1], -1.0,
                ALU.mult, ALU.mult),
                r=[("mcol", j), ("rstd", j)], w=[("nmr", j)])

    def LN_NORM(j, want_lo, tag):
        rk = ("resid", j)
        em.op("vector", lambda e: e.tensor_scalar(
            xn32[:, :], resid[:, j * D:(j + 1) * D],
            rstd[:, j:j + 1], nmr[:, j:j + 1], ALU.mult, ALU.add),
            r=[rk, ("rstd", j), ("nmr", j), ("xn32",)], w=[("xn32",)],
            tag=f"{tag}norm{j}")
        em.op("vector", lambda e: e.tensor_copy(hi_t[:, :], xn32[:, :]),
              r=[("xn32",)], w=HIK)
        if want_lo:
            em.op("vector", lambda e: e.tensor_tensor(
                lofm[:, :], xn32[:, :], hi_t[:, :], ALU.subtract),
                r=[("xn32",)] + HIK, w=LOK)
        for half in range(2):
            for cc in range(8):
                c = half * 8 + cc
                em.op("tensor", lambda e, c=c, cc=cc: e.transpose(
                    psT[:, cc * 128:(cc + 1) * 128],
                    hi_t[:, c * 128:(c + 1) * 128], identb[:, :]),
                    r=HIK + [("identb",)], w=[("psT",)])
            xv = xnfm[:, :].rearrange("p (c t) -> p c t", c=16)
            em.op("vector", lambda e, half=half: e.tensor_copy(
                xv[:, half * 8:(half + 1) * 8, j * 128:(j + 1) * 128],
                psT[:, :].rearrange("p (c t) -> p c t", c=8)),
                r=[("psT",)],
                w=_gk("xnfm", half * 8 * XG, (half + 1) * 8 * XG, XG))
        if want_lo:
            for half in range(2):
                for cc in range(8):
                    c = half * 8 + cc
                    em.op("tensor", lambda e, c=c, cc=cc: e.transpose(
                        psT[:, cc * 128:(cc + 1) * 128],
                        lofm[:, c * 128:(c + 1) * 128], identb[:, :]),
                        r=LOK + [("identb",)], w=[("psT",)])
                em.op("vector", lambda e, half=half: e.tensor_copy(
                    lofm[:, half * 1024:(half + 1) * 1024], psT[:, :]),
                    r=[("psT",)], w=LOK)

    def ROUTER_TILE(j, tag):
        logit_ps = psF
        hkeys = _gk("xnfm", 0, 16 * XG, XG)
        nmm = 16 * 3
        k = 0
        for c in range(16):
            hi_ap = xnfm[:, c * 1024 + j * 128: c * 1024 + (j + 1) * 128]
            lo_ap = lofm[:, c * 128:(c + 1) * 128]
            rhi = rthi[:, c * 8:(c + 1) * 8]
            rlo = rtlo[:, c * 8:(c + 1) * 8]
            for lhs_ap, rhs_ap, lk, rk2 in (
                    (hi_ap, rhi, hkeys[c * 1024 // XG], ("rthi",)),
                    (hi_ap, rlo, hkeys[c * 1024 // XG], ("rtlo",)),
                    (lo_ap, rhi, LOK[0], ("rthi",))):
                em.op("tensor", lambda e, l=lhs_ap, rr=rhs_ap, s=(k == 0),
                      p=(k == nmm - 1): e.matmul(
                          logit_ps[:, 0:8], l, rr, start=s, stop=p),
                      r=[lk, rk2], w=[("psF",)], tag=f"{tag}lg{j}")
                k += 1
        em.op("vector", lambda e: e.reduce_max(
            nmx[:, :], logit_ps[:, 0:8], axis=AX.X, negate=True),
            r=[("psF",)], w=[("nmx",)])
        em.op("scalar", lambda e: e.activation(
            out=probs8[:, :], in_=logit_ps[:, 0:8], func=AF.Exp,
            bias=nmx[:, :], accum_out=se_all[:, j:j + 1]),
            r=[("psF",), ("nmx",)], w=[("probs8",), ("se", j)], tag=f"{tag}exp{j}")
        em.op("vector", lambda e: e.reduce_max(
            pv_all[:, 2 * j:2 * j + 1], probs8[:, :], axis=AX.X),
            r=[("probs8",)], w=[("pv", j)])
        em.op("vector", lambda e: e.tensor_scalar(
            msk8[:, :], probs8[:, :], pv_all[:, 2 * j:2 * j + 1], None,
            ALU.is_equal), r=[("probs8",), ("pv", j)], w=[("msk8",)])
        em.op("vector", lambda e: e.select(
            tmp8[:, :], msk8[:, :], iota8[:, :], c8[:, :]),
            r=[("msk8",), ("iota8",), ("c8",)], w=[("tmp8",)])
        em.op("vector", lambda e: e.tensor_reduce(
            idx1[:, :], tmp8[:, :], axis=AX.X, op=ALU.min),
            r=[("tmp8",)], w=[("idx1",)])
        oh1 = oh_all[:, j * 16:j * 16 + 8]
        oh2 = oh_all[:, j * 16 + 8:j * 16 + 16]
        em.op("vector", lambda e: e.tensor_scalar(
            oh1, iota8[:, :], idx1[:, :], None, ALU.is_equal),
            r=[("iota8",), ("idx1",)], w=[("oh", j)])
        em.op("vector", lambda e: e.scalar_tensor_tensor(
            pm8[:, :], oh1, -1e30, probs8[:, :], ALU.mult, ALU.add),
            r=[("oh", j), ("probs8",)], w=[("pm8",)])
        em.op("vector", lambda e: e.reduce_max(
            pv_all[:, 2 * j + 1:2 * j + 2], pm8[:, :], axis=AX.X),
            r=[("pm8",)], w=[("pv", j)])
        em.op("vector", lambda e: e.tensor_scalar(
            msk8[:, :], pm8[:, :], pv_all[:, 2 * j + 1:2 * j + 2], None,
            ALU.is_equal), r=[("pm8",), ("pv", j)], w=[("msk8",)])
        em.op("vector", lambda e: e.select(
            tmp8[:, :], msk8[:, :], iota8[:, :], c8[:, :]),
            r=[("msk8",), ("iota8",), ("c8",)], w=[("tmp8",)])
        em.op("vector", lambda e: e.tensor_reduce(
            idx2[:, :], tmp8[:, :], axis=AX.X, op=ALU.min),
            r=[("tmp8",)], w=[("idx2",)])
        em.op("vector", lambda e: e.tensor_scalar(
            oh2, iota8[:, :], idx2[:, :], None, ALU.is_equal),
            r=[("iota8",), ("idx2",)], w=[("oh", j)])
        if DEBUG_ROUTER:
            em.op("vector", lambda e: e.tensor_copy(
                dbgsb[:, j * 8 + 2:j * 8 + 3], idx1[:, :]),
                r=[("idx1",)], w=[("dbgsb",)])
            em.op("vector", lambda e: e.tensor_copy(
                dbgsb[:, j * 8 + 3:j * 8 + 4], idx2[:, :]),
                r=[("idx2",)], w=[("dbgsb",)])
        em.op("tensor", lambda e: e.matmul(
            psF[0:8, 16 + j:17 + j], oh1, onescol[:, :], start=True, stop=True),
            r=[("oh", j), ("onescol",)], w=[("psF",)])
        em.op("tensor", lambda e: e.matmul(
            psF[0:8, 24 + j:25 + j], oh2, onescol[:, :], start=True, stop=True),
            r=[("oh", j), ("onescol",)], w=[("psF",)])
        em.op("tensor", lambda e: e.matmul(
            psG[:, j * 16:j * 16 + 8], u128[:, :], oh1, start=False, stop=False),
            r=[("u128",), ("oh", j)], w=[("psG",)])
        em.op("tensor", lambda e: e.matmul(
            psG[:, j * 16 + 8:j * 16 + 16], u128[:, :], oh2, start=False,
            stop=False),
            r=[("u128",), ("oh", j)], w=[("psG",)])

    def ROUTER_GLOBAL1(call, tag):
        em.op("vector", lambda e: e.tensor_copy(cnt16[:, :], psF[0:8, 16:32]),
              r=[("psF",)], w=[("cnt16",)])
        em.op("vector", lambda e: e.reduce_sum(
            cntsb[:, 0:1], cnt16[:, 0:8], axis=AX.X),
            r=[("cnt16",)], w=[("cntsb",)])
        em.op("vector", lambda e: e.reduce_sum(
            cntsb[:, 1:2], cnt16[:, 8:16], axis=AX.X),
            r=[("cnt16",)], w=[("cntsb",)])
        em.op("sync", lambda e: e.dma_start(out=cc_in[call].ap(), in_=cntsb[:, :]),
              dma=True, r=[("cntsb",)], w=[("d:ccin", call)])
        for k in range(2):
            em.op("tensor", lambda e, k=k: e.transpose(
                psF[0:8, 40:48], cnt16[:, 8 * k:8 * k + 8], identf[0:8, 0:8]),
                r=[("cnt16",), ("identf",)], w=[("psF",)])
            em.op("vector", lambda e, k=k: e.tensor_copy(
                cntT[:, 8 * k:8 * k + 8], psF[0:8, 40:48]),
                r=[("psF",)], w=[("cntT",)])
            em.op("tensor", lambda e, k=k: e.matmul(
                psF[0:8, 48 + 8 * k:56 + 8 * k], cntT[:, 8 * k:8 * k + 8],
                u128[0:8, 0:8], start=True, stop=True),
                r=[("cntT",), ("u128",)], w=[("psF",)])
        em.op("gpsimd", lambda e: e.collective_compute(
            "AllGather", ALU.bypass, replica_groups=[list(range(8))],
            ins=[cc_in[call].ap().opt()], outs=[cc_out[call].ap().opt()]),
            r=[("d:ccin", call)], w=[("d:ccout", call)], tag=f"{tag}AG")
        em.op("sync", lambda e: e.dma_start(out=gath[:, :], in_=cc_out[call].ap()),
              dma=True, r=[("d:ccout", call)], w=[("gath",)])

    def ROUTER_GLOBAL2(call, tag):
        em.op("tensor", lambda e: e.matmul(
            psF[0:8, 64:65], selm[:, 0:8], gath[:, 0:1], start=True, stop=True),
            r=[("selm",), ("gath",)], w=[("psF",)], tag=f"{tag}base")
        em.op("tensor", lambda e: e.matmul(
            psF[0:8, 65:66], selm[:, 8:16], gath[:, 0:1], start=True, stop=False),
            r=[("selm",), ("gath",)], w=[("psF",)])
        em.op("tensor", lambda e: e.matmul(
            psF[0:8, 65:66], selm[:, 0:8], gath[:, 1:2], start=False, stop=True),
            r=[("selm",), ("gath",)], w=[("psF",)])
        em.op("vector", lambda e: e.tensor_copy(bsb[:, :], psF[0:8, 64:66]),
              r=[("psF",)], w=[("bsb",)])
        for k in range(2):
            em.op("vector", lambda e, k=k: e.tensor_scalar(
                comb[:, 8 * k:8 * k + 8], psF[0:8, 48 + 8 * k:56 + 8 * k],
                bsb[:, k:k + 1], None, ALU.add),
                r=[("psF",), ("bsb",)], w=[("comb",)])
        if DEBUG_ROUTER:
            em.op("vector", lambda e: e.tensor_copy(
                dbgsb2[:, 0:128], psG[:, 0:128]),
                r=[("psG",)], w=[("dbgsb2",)])
            em.op("vector", lambda e: e.tensor_copy(
                dbgsb2[0:8, 128:144], comb[:, :]),
                r=[("comb",)], w=[("dbgsb2",)])
            em.op("vector", lambda e: e.tensor_copy(
                dbgsb2[:, 144:272], oh_all[:, 0:128]),
                r=[("oh", jj) for jj in range(NT)], w=[("dbgsb2",)])
        for j in range(NT):
            for k in range(2):
                em.op("tensor", lambda e, j=j, k=k: e.transpose(
                    psF[0:1, 72:80], comb[:, 8 * k + j:8 * k + j + 1],
                    identf[0:8, 0:8]),
                    r=[("comb",), ("identf",)], w=[("psF",)])
                em.op("vector", lambda e, j=j, k=k: e.tensor_copy(
                    b0row[:, :], psF[0:1, 72:80]),
                    r=[("psF",)], w=[("b0row",)])
                em.op("tensor", lambda e, j=j, k=k: e.matmul(
                    psG[:, j * 16 + 8 * k:j * 16 + 8 * k + 8], ones1[:, :],
                    b0row[:, :], start=False,
                    stop=(j == NT - 1 and k == 1)),
                    r=[("ones1",), ("b0row",)], w=[("psG",)])
        for j in range(NT):
            for k in range(2):
                dst = sc1 if k == 0 else sc2
                dk = ("sc1",) if k == 0 else ("sc2",)
                em.op("vector", lambda e, j=j, k=k, dst=dst:
                      e.scalar_tensor_tensor(
                          tmp8[:, :], psG[:, j * 16 + 8 * k:j * 16 + 8 * k + 8],
                          1.0, oh_all[:, j * 16 + 8 * k:j * 16 + 8 * k + 8],
                          ALU.mult, ALU.mult, accum_out=dst[:, :]),
                      r=[("psG",), ("oh", j)], w=[("tmp8",), dk],
                      tag=f"{tag}pos{j}")
                if DEBUG_ROUTER:
                    em.op("vector", lambda e, j=j, k=k, dst=dst: e.tensor_copy(
                        dbgsb[:, j * 8 + k:j * 8 + k + 1], dst[:, :]),
                        r=[dk], w=[("dbgsb",)])
                em.op("vector", lambda e, j=j, k=k, dst=dst: e.tensor_scalar(
                    dst[:, :], dst[:, :], CAP,
                    pv_all[:, 2 * j + k:2 * j + k + 1], ALU.is_lt, ALU.mult),
                    r=[dk, ("pv", j)], w=[dk])
            em.op("vector", lambda e, j=j: e.tensor_tensor(
                sc3[:, :], sc1[:, :], sc2[:, :], ALU.add),
                r=[("sc1",), ("sc2",)], w=[("sc3",)])
            em.op("vector", lambda e, j=j: e.reciprocal(
                rc1[:, :], se_all[:, j:j + 1]),
                r=[("se", j)], w=[("rc1",)])
            em.op("vector", lambda e, j=j: e.tensor_scalar(
                scol[:, j:j + 1], sc3[:, :], rc1[:, :], 0.5, ALU.mult, ALU.mult),
                r=[("sc3",), ("rc1",)], w=[("scol", j)], tag=f"{tag}s{j}")
            if DEBUG_ROUTER and call == 0:
                em.op("vector", lambda e, j=j: e.tensor_copy(
                    dbgsb[:, j * 8 + 4:j * 8 + 6],
                    pv_all[:, 2 * j:2 * j + 2]),
                    r=[("pv", j)], w=[("dbgsb",)])
                em.op("vector", lambda e, j=j: e.tensor_copy(
                    dbgsb[:, j * 8 + 6:j * 8 + 7], se_all[:, j:j + 1]),
                    r=[("se", j)], w=[("dbgsb",)])
                em.op("vector", lambda e, j=j: e.tensor_copy(
                    dbgsb[:, j * 8 + 7:j * 8 + 8], scol[:, j:j + 1]),
                    r=[("scol", j)], w=[("dbgsb",)])

    xkeys = _gk("xnfm", 0, 16 * XG, XG)

    def W1(tt, tag, call=0):
        for s in range(NS):
            buf = wb[s % 2]
            bk = [("wb", s % 2, 0), ("wb", s % 2, 1)]
            if call == 0 and tt == 0:
                em.op("gpsimd", lambda e, s=s, buf=buf: e.dma_start(
                    out=buf[:, :], in_=w1s[s, :, :, :].rearrange("p c f -> p (c f)")),
                    dma=True, w=bk, tag=f"{tag}w1d{s}")
                em.op("sync", lambda e, s=s, buf=buf: e.dma_start(
                    out=w1bf.ap()[s, :, :], in_=buf[:, :]),
                    dma=True, r=bk, w=[("d:w1bf", s)], tag=f"{tag}w1st{s}")
            else:
                em.op("gpsimd", lambda e, s=s, buf=buf: e.dma_start(
                    out=buf[:, :], in_=w1bf.ap()[s, :, :]),
                    dma=True, r=[("d:w1bf", s)], w=bk, tag=f"{tag}w1d{s}")
            ps = PS[s % 2]
            pk = PSK[s % 2]
            for c in range(16):
                em.op("tensor", lambda e, s=s, c=c, tt=tt, buf=buf, ps=ps:
                      e.matmul(ps[:, :],
                               buf[:, c * 128:(c + 1) * 128],
                               xnfm[:, c * 1024 + tt * 512:c * 1024 + (tt + 1) * 512],
                               start=(c == 0), stop=(c == 15)),
                      r=bk + [xkeys[c]], w=[(pk,)], tag=f"{tag}w1mm")
            em.op("scalar", lambda e, s=s, ps=ps: e.activation(
                out=big[:, s * 512:(s + 1) * 512], in_=ps[:, :],
                func=AF.Gelu, bias=b1sb[:, s:s + 1]),
                r=[(pk,), ("b1sb",)], w=_gk("big", s * 512, (s + 1) * 512, BG),
                tag=f"{tag}gelu{s}")

    def W2(tt, tag, call=0, pre_evac_hook=None):
        for q in range(4):
            banks = [0, 1, 4, 5] if q % 2 == 0 else [6, 0, 1, 4]
            qs = [PS[b] for b in banks]
            qk = [PSK[b] for b in banks]
            for s in range(NS):
                slot = s % 4
                buf = wb[slot % 2]
                half = slot // 2
                bk = ("wb", slot % 2, half)
                dst = buf[:, half * 1024 + 256:half * 1024 + 768] if False else buf[:, half * 1024:half * 1024 + 512]
                if call == 0 and tt == 0:
                    em.op("gpsimd", lambda e, q=q, s=s, dst=dst: e.dma_start(
                        out=dst, in_=w2s[q, s * 128:(s + 1) * 128, :]),
                        dma=True, w=[bk], tag=f"{tag}w2d")
                    em.op("sync", lambda e, q=q, s=s, dst=dst: e.dma_start(
                        out=w2bf.ap()[q, s, :, :], in_=dst),
                        dma=True, r=[bk], w=[("d:w2bf", q, s)],
                        tag=f"{tag}w2st")
                else:
                    em.op("gpsimd", lambda e, q=q, s=s, dst=dst: e.dma_start(
                        out=dst, in_=w2bf.ap()[q, s, :, :]),
                        dma=True, r=[("d:w2bf", q, s)], w=[bk], tag=f"{tag}w2d")
                for u in range(4):
                    em.op("tensor", lambda e, s=s, u=u, dst=dst, ps=qs[u]:
                          e.matmul(ps[:, :],
                                   big[:, s * 512 + u * 128:s * 512 + (u + 1) * 128],
                                   dst[:, 0:512],
                                   start=(s == 0), stop=False),
                          r=[bk] + _gk("big", s * 512, (s + 1) * 512, BG),
                          w=[(qk[u],)], tag=f"{tag}w2mm")
            if q == 0 and pre_evac_hook:
                pre_evac_hook()
            for u in range(4):
                em.op("tensor", lambda e, q=q, u=u, ps=qs[u]: e.matmul(
                    ps[:, :], ones1b[:, :], b2sb[0:1, q * 512:(q + 1) * 512],
                    start=False, stop=True),
                    r=[("ones1b",), ("b2sb",)], w=[(qk[u],)], tag=f"{tag}b2")
            for u in range(4):
                j = tt * 4 + u
                rs = resid[:, j * D + q * 512:j * D + (q + 1) * 512]
                em.op("vector", lambda e, j=j, q=q, u=u, ps=qs[u], rs=rs:
                      e.scalar_tensor_tensor(
                          rs, ps[:, :], scol[:, j:j + 1], rs,
                          ALU.mult, ALU.add,
                          accum_out=accs[:, j * 4 + q:j * 4 + q + 1]),
                      r=[(qk[u],), ("scol", j), ("resid", j)],
                      w=[("resid", j), ("accs", j)], tag=f"{tag}ev{j}q{q}")

    def FFN(call, tag, mid_hook=None, end_hook=None, skip_stats=False):
        sum_from_accs = (call > 0)
        for z in range(4):
            em.op("tensor", lambda e, z=z: e.matmul(
                psG[:, z * 128:(z + 1) * 128], zrow[:, :], identb[0:1, 0:128],
                start=(z == 0), stop=False),
                r=[("zrow",), ("identb",)], w=[("psG",)], tag=f"{tag}psCclr")
        if not skip_stats:
            for j in range(NT):
                LN_STATS(j, sum_from_accs, f"{tag}ln")
            LN_RSTD(list(range(NT)), f"{tag}ln")
        for j in range(4):
            LN_NORM(j, True, f"{tag}ln")
            ROUTER_TILE(j, tag)
        W1(0, tag, call=call)
        for j in range(4, NT):
            LN_NORM(j, True, f"{tag}ln")
            ROUTER_TILE(j, tag)
        cc = 0 if call == 0 else 1
        ROUTER_GLOBAL1(cc, tag)
        ROUTER_GLOBAL2(cc, tag)
        W2(0, tag, call=call)
        if mid_hook:
            mid_hook()
        W1(1, tag, call=call)
        W2(1, tag, call=call)
        if end_hook:
            end_hook()

    def ATTN(tag, want_ln3):
        qfull = _gk("xnfm", 0, 16 * XG, XG)
        ckvfull = _gk("big", 24576, 32768, BG)
        QH0, KH0, VH0 = 16384, 18432, 20480
        qhk = _gk("big", QH0, QH0 + 2048, BG)
        khk = _gk("big", KH0, KH0 + 2048, BG)
        vhk = _gk("big", VH0, VH0 + 2048, BG)

        def QLAT_CKV(th):
            for r in range(8):
                buf = wb[r % 2]
                bk = [("wb", r % 2, 0), ("wb", r % 2, 1)]
                em.op("gpsimd", lambda e, r=r, buf=buf: e.dma_start(
                    out=buf[:, :], in_=wqds[r, :, :, :].rearrange("p c f -> p (c f)")),
                    dma=True, w=bk, tag=f"{tag}qd{r}")
                ps, pk = PS[r % 2], PSK[r % 2]
                for c in range(16):
                    em.op("tensor", lambda e, r=r, th=th, c=c, buf=buf, ps=ps:
                          e.matmul(ps[:, :], buf[:, c * 128:(c + 1) * 128],
                                   xnfm[:, c * 1024 + th * 512:c * 1024 + (th + 1) * 512],
                                   start=(c == 0), stop=(c == 15)),
                          r=bk + [xkeys[c]], w=[(pk,)], tag=f"{tag}qlmm")
                em.op("scalar", lambda e, r=r, th=th, ps=ps: e.activation(
                    out=big[:, r * 1024 + th * 512:r * 1024 + (th + 1) * 512],
                    in_=ps[:, :], func=AF.Copy),
                    r=[(pk,)], w=_gk("big", r * 1024 + th * 512,
                                     r * 1024 + (th + 1) * 512, BG),
                    tag=f"{tag}qlev")
            for r in range(4):
                buf = wb[r % 2]
                bk = [("wb", r % 2, 0), ("wb", r % 2, 1)]
                em.op("gpsimd", lambda e, r=r, buf=buf: e.dma_start(
                    out=buf[:, :], in_=wkvds[r, :, :, :].rearrange("p c f -> p (c f)")),
                    dma=True, w=bk, tag=f"{tag}kvd{r}")
                ps, pk = PS[2 + r % 2], PSK[2 + r % 2]
                for c in range(16):
                    em.op("tensor", lambda e, r=r, th=th, c=c, buf=buf, ps=ps:
                          e.matmul(ps[:, :], buf[:, c * 128:(c + 1) * 128],
                                   xnfm[:, c * 1024 + th * 512:c * 1024 + (th + 1) * 512],
                                   start=(c == 0), stop=(c == 15)),
                          r=bk + [xkeys[c]], w=[(pk,)], tag=f"{tag}ckvmm")
                em.op("scalar", lambda e, r=r, th=th, ps=ps: e.activation(
                    out=big[:, 8192 + r * 1024 + th * 512:8192 + r * 1024 + (th + 1) * 512],
                    in_=ps[:, :], func=AF.Copy),
                    r=[(pk,)], w=_gk("big", 8192 + r * 1024 + th * 512,
                                     8192 + r * 1024 + (th + 1) * 512, BG),
                    tag=f"{tag}ckvev")

        for j in range(4):
            LN_NORM(j, False, f"{tag}ln")
        QLAT_CKV(0)
        for j in range(4, NT):
            LN_NORM(j, False, f"{tag}ln")
        QLAT_CKV(1)
        latvq = lat_in.ap()[0:1024, :].rearrange("(n p) f -> p n f", p=128)
        bigvq = big[:, 0:8 * 1024].rearrange("p (n f) -> p n f", f=1024)
        em.op("sync", lambda e: e.dma_start(out=latvq, in_=bigvq), dma=True,
              r=_gk("big", 0, 8192, BG), w=[("d:latinq",)], tag=f"{tag}latoutq")
        em.op("gpsimd", lambda e: e.collective_compute(
            "AllGather", ALU.bypass,
            replica_groups=[[0, 1], [2, 3], [4, 5], [6, 7]],
            ins=[lat_in.ap()[0:1024, :].opt()],
            outs=[lat_outq.ap().opt()]),
            r=[("d:latinq",)], w=[("d:latoutq",)], tag=f"{tag}AGlatq")
        latvk = lat_in.ap()[1024:1536, :].rearrange("(n p) f -> p n f", p=128)
        bigvk = big[:, 8192:12 * 1024].rearrange("p (n f) -> p n f", f=1024)
        em.op("sync", lambda e: e.dma_start(out=latvk, in_=bigvk), dma=True,
              r=_gk("big", 8192, 12288, BG), w=[("d:latink",)], tag=f"{tag}latoutk")
        em.op("gpsimd", lambda e: e.collective_compute(
            "AllGather", ALU.bypass,
            replica_groups=[[0, 1], [2, 3], [4, 5], [6, 7]],
            ins=[lat_in.ap()[1024:1536, :].opt()],
            outs=[lat_outk.ap().opt()]),
            r=[("d:latink",)], w=[("d:latoutk",)], tag=f"{tag}AGlatk")
        lovq = lat_outq.ap().rearrange("(w n p) f -> w n p f", w=2, p=128)
        lovk = lat_outk.ap().rearrange("(w n p) f -> w n p f", w=2, p=128)
        for r in range(8):
            for who in range(2):
                em.op("sync", lambda e, r=r, who=who: e.dma_start(
                    out=xnfm[:, r * 2048 + who * 1024:r * 2048 + (who + 1) * 1024],
                    in_=lovq[who, r, :, :]),
                    dma=True, r=[("d:latoutq",)],
                    w=_gk("xnfm", r * 2048 + who * 1024, r * 2048 + (who + 1) * 1024, XG),
                    tag=f"{tag}qfull")
        for r in range(4):
            for who in range(2):
                em.op("sync", lambda e, r=r, who=who: e.dma_start(
                    out=big[:, 24576 + r * 2048 + who * 1024:24576 + r * 2048 + (who + 1) * 1024],
                    in_=lovk[who, r, :, :]),
                    dma=True, r=[("d:latoutk",)],
                    w=_gk("big", 24576 + r * 2048 + who * 1024,
                          24576 + r * 2048 + (who + 1) * 1024, BG),
                    tag=f"{tag}ckvfull")

        MX = [mx4, mx4b]
        NMX = [nmx, nmxb]
        SUME = [sume, sumeb]
        SE1 = [se1, se1b]
        RC1 = [rc1, rc1b]
        ABUF = [(hi_t, ("hi_t",)), (lofm, ("lo_t",))]

        def srow(qt):
            klen = (qt + 1) * 128
            off = (qt % 2) * 1024 if qt <= 7 else 0
            keys = [f"psS{b}" for b in range(off // 512,
                                             (off + klen + 511) // 512)]
            return off, klen, keys

        def SCORES(h, qt):
            off, klen, keys = srow(qt)
            for kc in range((klen + 511) // 512):
                w_ = min(512, klen - kc * 512)
                em.op("tensor", lambda e, qt=qt, kc=kc, w_=w_, off=off:
                      e.matmul(psS[:, off + kc * 512:off + kc * 512 + w_],
                               big[:, QH0 + qt * 128:QH0 + (qt + 1) * 128],
                               big[:, KH0 + kc * 512:KH0 + kc * 512 + w_],
                               start=True, stop=True),
                      r=[qhk[(qt * 128) // BG], khk[(kc * 512) // BG]],
                      w=[(keys[kc],)], tag=f"{tag}sc")

        def SOFT(h, qt):
            off, klen, keys = srow(qt)
            kk = [(k,) for k in keys]
            p = qt % 2
            abuf, abk = ABUF[p]
            doff = off + qt * 128
            dk2 = (keys[(qt * 128) // 512],)
            em.op("vector", lambda e, doff=doff: e.tensor_tensor(
                psS[:, doff:doff + 128], psS[:, doff:doff + 128],
                cmask[:, :], ALU.add),
                r=[dk2, ("cmask",)], w=[dk2], tag=f"{tag}mask")
            em.op("vector", lambda e, off=off, klen=klen, p=p: e.reduce_max(
                NMX[p][:, :], psS[:, off:off + klen], axis=AX.X, negate=True),
                r=kk, w=[("nmx", p)], tag=f"{tag}max")
            em.op("scalar", lambda e, off=off, klen=klen, p=p, abuf=abuf:
                  e.activation(
                      out=abuf[:, 0:klen], in_=psS[:, off:off + klen],
                      func=AF.Exp, bias=NMX[p][:, :],
                      accum_out=SE1[p][:, :]),
                  r=kk + [("nmx", p)], w=[abk, ("se1", p)],
                  tag=f"{tag}exp")
            em.op("vector", lambda e, p=p: e.reciprocal(RC1[p][:, :], SE1[p][:, :]),
                  r=[("se1", p)], w=[("rc1", p)])
            em.op("vector", lambda e, p=p, abuf=abuf, klen=klen: e.tensor_scalar(
                abuf[:, 0:klen], abuf[:, 0:klen], RC1[p][:, :], None, ALU.mult),
                r=[abk, ("rc1", p)], w=[abk], tag=f"{tag}anorm")

        def CTX(h, qt):
            p = qt % 2
            abuf, abk = ABUF[p]
            cps, cpk = (psE, "psE") if p == 0 else (psF, "psF")
            for g in range((qt + 1 + 3) // 4):
                n_in_g = min(4, qt + 1 - g * 4)
                gp = g % 2
                tb = ("psT",)
                for kk in range(n_in_g):
                    kt = g * 4 + kk
                    em.op("tensor", lambda e, kt=kt, kk=kk, gp=gp, abuf=abuf:
                          e.transpose(
                              psT[:, gp * 512 + kk * 128:gp * 512 + (kk + 1) * 128],
                              abuf[:, kt * 128:(kt + 1) * 128], identb[:, :]),
                          r=[abk, ("identb",)], w=[tb], tag=f"{tag}aT")
                em.op("vector", lambda e, n_in_g=n_in_g, gp=gp: e.tensor_copy(
                    attnT[:, gp * 512:gp * 512 + n_in_g * 128],
                    psT[:, gp * 512:gp * 512 + n_in_g * 128]),
                    r=[tb], w=[("attnT", gp)])
                for kk in range(n_in_g):
                    kt = g * 4 + kk
                    em.op("tensor", lambda e, kt=kt, kk=kk, qt=qt, cps=cps, gp=gp:
                          e.matmul(cps[:, 0:128],
                                   big[:, VH0 + kt * 128:VH0 + (kt + 1) * 128],
                                   attnT[:, gp * 512 + kk * 128:gp * 512 + (kk + 1) * 128],
                                   start=(kt == 0), stop=(kt == qt)),
                          r=[vhk[(kt * 128) // BG], ("attnT", gp)],
                          w=[(cpk,)], tag=f"{tag}ctxmm")
            em.op("vector", lambda e, h=h, qt=qt, cps=cps: e.tensor_copy(
                big[:, h * 2048 + qt * 128:h * 2048 + (qt + 1) * 128],
                cps[:, 0:128]),
                r=[(cpk,)],
                w=[_gk("big", h * 2048 + qt * 128, h * 2048 + (qt + 1) * 128, BG)[0]],
                tag=f"{tag}ctxev")

        for h in range(NH):
            buf = wb[h % 2]
            bk = [("wb", h % 2, 0), ("wb", h % 2, 1)]
            em.op("gpsimd", lambda e, h=h, buf=buf: e.dma_start(
                out=buf[:, :], in_=wqkv[h, :, :]), dma=True, w=bk,
                tag=f"{tag}whd{h}")
            for tc in range(4):
                ps, pk = PS[tc], PSK[tc]
                for r in range(8):
                    em.op("tensor", lambda e, h=h, tc=tc, r=r, buf=buf, ps=ps:
                          e.matmul(ps[:, :], buf[:, r * 128:(r + 1) * 128],
                                   xnfm[:, r * 2048 + tc * 512:r * 2048 + (tc + 1) * 512],
                                   start=(r == 0), stop=(r == 7)),
                          r=bk + [qfull[(r * 2048 + tc * 512) // XG]],
                          w=[(pk,)], tag=f"{tag}qhmm")
                em.op("scalar", lambda e, tc=tc, ps=ps: e.activation(
                    out=big[:, QH0 + tc * 512:QH0 + (tc + 1) * 512], in_=ps[:, :],
                    func=AF.Copy, scale=c_qs[:, :]),
                    r=[(pk,), ("c_qs",)], w=[qhk[(tc * 512) // BG]],
                    tag=f"{tag}qhev")
            for tc in range(4):
                ps, pk = PS[tc], PSK[tc]
                for r in range(4):
                    em.op("tensor", lambda e, h=h, tc=tc, r=r, buf=buf, ps=ps:
                          e.matmul(ps[:, :], buf[:, 1024 + r * 128:1024 + (r + 1) * 128],
                                   big[:, 24576 + r * 2048 + tc * 512:24576 + r * 2048 + (tc + 1) * 512],
                                   start=(r == 0), stop=(r == 3)),
                          r=bk + [ckvfull[(r * 2048 + tc * 512) // BG]],
                          w=[(pk,)], tag=f"{tag}khmm")
                em.op("scalar", lambda e, tc=tc, ps=ps: e.activation(
                    out=big[:, KH0 + tc * 512:KH0 + (tc + 1) * 512], in_=ps[:, :],
                    func=AF.Copy),
                    r=[(pk,)], w=[khk[(tc * 512) // BG]], tag=f"{tag}khev")
            for kt in range(16):
                ps, pk = (psE, "psE") if kt % 2 == 0 else (psF, "psF")
                for r in range(4):
                    em.op("tensor", lambda e, kt=kt, r=r, buf=buf, ps=ps:
                          e.matmul(ps[:, 0:128],
                                   big[:, 24576 + r * 2048 + kt * 128:24576 + r * 2048 + (kt + 1) * 128],
                                   buf[:, 1536 + r * 128:1536 + (r + 1) * 128],
                                   start=(r == 0), stop=(r == 3)),
                          r=bk + [ckvfull[(r * 2048 + kt * 128) // BG]],
                          w=[(pk,)], tag=f"{tag}vhmm")
                em.op("vector", lambda e, kt=kt, ps=ps: e.tensor_copy(
                    big[:, VH0 + kt * 128:VH0 + (kt + 1) * 128], ps[:, 0:128]),
                    r=[(pk,)], w=[vhk[(kt * 128) // BG]], tag=f"{tag}vhev")
            SCORES(h, 0)
            for qt in range(16):
                if qt < 15 and qt + 1 <= 7:
                    SCORES(h, qt + 1)
                    SOFT(h, qt)
                else:
                    SOFT(h, qt)
                    if qt < 15:
                        SCORES(h, qt + 1)
                CTX(h, qt)
            for d in range(2):
                row0 = (h // 4) * 1024 + d * 512 + (h % 4) * 128
                civ = ctx_in.ap()[row0:row0 + 128, :]
                em.op("sync", lambda e, h=h, d=d, civ=civ: e.dma_start(
                    out=civ,
                    in_=big[:, h * 2048 + d * 1024:h * 2048 + (d + 1) * 1024]),
                    dma=True,
                    r=_gk("big", h * 2048 + d * 1024, h * 2048 + (d + 1) * 1024, BG),
                    w=[("d:ctxin", h, d)], tag=f"{tag}ctxout{h}")
            if h == 3:
                em.op("gpsimd", lambda e: e.collective_compute(
                    "AllGather", ALU.bypass,
                    replica_groups=[[0, 1], [2, 3], [4, 5], [6, 7]],
                    ins=[ctx_in.ap()[0:1024, :].opt()],
                    outs=[ctx_outA.ap().opt()]),
                    r=[("d:ctxin", h2, d2) for h2 in range(4) for d2 in range(2)],
                    w=[("d:ctxoutA",)], tag=f"{tag}AGctxA")
        em.op("gpsimd", lambda e: e.collective_compute(
            "AllGather", ALU.bypass,
            replica_groups=[[0, 1], [2, 3], [4, 5], [6, 7]],
            ins=[ctx_in.ap()[1024:2048, :].opt()],
            outs=[ctx_outB.ap().opt()]),
            r=[("d:ctxin", h2, d2) for h2 in range(4, 8) for d2 in range(2)],
            w=[("d:ctxoutB",)], tag=f"{tag}AGctxB")
        wosk = _gk("big", 0, 16384, BG)
        for c in range(16):
            em.op("gpsimd", lambda e, c=c: e.dma_start(
                out=big[:, c * 1024:(c + 1) * 1024],
                in_=wos[c, :, 0:1024]),
                dma=True, w=wosk[c:c + 1], tag=f"{tag}wod0")
        for cc in range(16):
            grp = cc // 8
            c = CORDER[cc]
            src_t = ctx_outA if grp == 0 else ctx_outB
            gk_ = ("d:ctxoutA",) if grp == 0 else ("d:ctxoutB",)
            em.op("gpsimd", lambda e, c=c, src_t=src_t: e.indirect_dma_start(
                out=big[:, 16384 + c * 1024:16384 + (c + 1) * 1024],
                out_offset=None,
                in_=src_t.ap()[0:128, :],
                in_offset=bass.IndirectOffsetOnAxis(ap=ctxidx[:, c:c + 1], axis=0),
                bounds_check=2048),
                dma=True, r=[gk_, ("ctxidx",)],
                w=_gk("big", 16384 + c * 1024, 16384 + (c + 1) * 1024, BG),
                tag=f"{tag}ctxall")
        ctxall = _gk("big", 16384, 32768, BG)
        for dh in range(2):
            if dh == 1:
                for c in range(16):
                    em.op("gpsimd", lambda e, c=c: e.dma_start(
                        out=big[:, c * 1024:(c + 1) * 1024],
                        in_=wos[c, :, 1024:2048]),
                        dma=True, w=wosk[c:c + 1], tag=f"{tag}wod")
            for jg in range(4):
                banks = [0, 1, 2, 3] if jg % 2 == 0 else [4, 5, 6, 2]
                for cc in range(16):
                    c = CORDER[cc]
                    for u in range(2):
                        j = jg * 2 + u
                        for i in range(2):
                            ps, pk = PS[banks[u * 2 + i]], PSK[banks[u * 2 + i]]
                            em.op("tensor", lambda e, c=c, j=j, i=i, ps=ps,
                                  cc=cc: e.matmul(
                                      ps[:, 0:512],
                                      big[:, 16384 + c * 1024 + j * 128:16384 + c * 1024 + (j + 1) * 128],
                                      big[:, c * 1024 + i * 512:c * 1024 + (i + 1) * 512],
                                      start=(cc == 0), stop=False),
                                  r=[wosk[c], ctxall[(c * 1024 + j * 128) // BG]],
                                  w=[(pk,)], tag=f"{tag}womm")
                for u in range(2):
                    j = jg * 2 + u
                    for i in range(2):
                        dc = dh * 2 + i
                        ps, pk = PS[banks[u * 2 + i]], PSK[banks[u * 2 + i]]
                        em.op("tensor", lambda e, dc=dc, ps=ps: e.matmul(
                            ps[:, 0:512], ones1b[:, :],
                            bosb[0:1, dc * 512:(dc + 1) * 512],
                            start=False, stop=True),
                            r=[("ones1b",), ("bosb",)], w=[(pk,)], tag=f"{tag}bo")
                        rs = resid[:, j * D + dc * 512:j * D + (dc + 1) * 512]
                        em.op("vector", lambda e, j=j, dc=dc, ps=ps, rs=rs:
                              e.scalar_tensor_tensor(
                                  rs, ps[:, 0:512], 1.0, rs, ALU.mult, ALU.add,
                                  accum_out=accs[:, j * 4 + dc:j * 4 + dc + 1]),
                              r=[(pk,), ("resid", j)],
                              w=[("resid", j), ("accs", j)], tag=f"{tag}woev{j}")
                    if dh == 1 and want_ln3:
                        LN_STATS(j, True, "F3ln")
        if want_ln3:
            LN_RSTD(list(range(NT)), "F3ln")

    def P_OUT():
        if DEBUG_ROUTER:
            em.op("sync", lambda e: e.dma_start(out=dbg, in_=dbgsb[:, :]),
                  dma=True, r=[("dbgsb",)], w=[("d:dbg",)])
            em.op("sync", lambda e: e.dma_start(out=dbg2, in_=dbgsb2[:, :]),
                  dma=True, r=[("dbgsb2",)], w=[("d:dbg2",)])
        for j in range(NT):
            em.op("sync", lambda e, j=j: e.dma_start(
                out=yout[j * 128:(j + 1) * 128, :], in_=resid[:, j * D:(j + 1) * D]),
                dma=True, r=[("resid", j)], w=[("d:y", j)], tag=f"out{j}")

    P0()

    def ln2_mid():
        for j in range(4):
            LN_STATS(j, True, "A2ln")

    def ln2_end():
        for j in range(4, NT):
            LN_STATS(j, True, "A2ln")
        LN_RSTD(list(range(NT)), "A2ln")

    if n_sub >= 2:
        FFN(0, "F1", mid_hook=ln2_mid, end_hook=ln2_end)
        ATTN("A", want_ln3=(n_sub >= 3))
        if n_sub >= 3:
            FFN(1, "F3", skip_stats=True)
    else:
        FFN(0, "F1")
    P_OUT()
    em.finalize(nc)
    stack.close()
    return nc


def _host_layouts(L0_attn_Wq_down, L0_attn_Wkv_down, L0_attn_Wq_up,
                  L0_attn_Wk_up, L0_attn_Wv_up, L0_attn_Wo, L0_attn_bo,
                  L0_ffn_router_w, L0_ffn_w1, L0_ffn_b1, L0_ffn_w2, L0_ffn_b2):
    c = np.ascontiguousarray
    w1T = c(np.asarray(L0_ffn_w1).T)
    w1s = c(w1T.reshape(16, 128, 64, 128).transpose(2, 1, 0, 3))
    w2s = c(np.asarray(L0_ffn_w2).T.reshape(8192, 4, 512).transpose(1, 0, 2))
    b1s = c(np.asarray(L0_ffn_b1).reshape(64, 128).T)
    b2row = c(np.asarray(L0_ffn_b2)[None, :])
    rT = c(np.asarray(L0_ffn_router_w).T.reshape(16, 128, 8).transpose(1, 0, 2)
           .reshape(128, 128))
    wqds = c(np.asarray(L0_attn_Wq_down).T.reshape(16, 128, 8, 128)
             .transpose(2, 1, 0, 3))
    wkvds = c(np.asarray(L0_attn_Wkv_down).T.reshape(16, 128, 4, 128)
              .transpose(2, 1, 0, 3))
    wquT = np.asarray(L0_attn_Wq_up).T.reshape(8, 128, 16, 128).transpose(2, 0, 1, 3)
    wkuT = np.asarray(L0_attn_Wk_up).T.reshape(4, 128, 16, 128).transpose(2, 0, 1, 3)
    wvuT = np.asarray(L0_attn_Wv_up).T.reshape(4, 128, 16, 128).transpose(2, 0, 1, 3)
    wqkv = np.concatenate([
        wquT.transpose(0, 2, 1, 3).reshape(16, 128, 1024),
        wkuT.transpose(0, 2, 1, 3).reshape(16, 128, 512),
        wvuT.transpose(0, 2, 1, 3).reshape(16, 128, 512)], axis=2)
    wqkv = c(wqkv)
    wos = c(np.asarray(L0_attn_Wo).T.reshape(16, 128, 2048))
    borow = c(np.asarray(L0_attn_bo)[None, :])
    return dict(w1s=w1s, w2s=w2s, b1s=b1s, b2row=b2row, rT=rT, wqds=wqds,
                wkvds=wkvds, wqkv=wqkv, wos=wos, borow=borow)


_cache = {}


def kernel(L0_attn_x, L0_attn_Wq_down, L0_attn_Wkv_down, L0_attn_Wq_up,
           L0_attn_Wk_up, L0_attn_Wv_up, L0_attn_Wo, L0_attn_bo,
           L0_ffn_router_w, L0_ffn_w1, L0_ffn_b1, L0_ffn_w2, L0_ffn_b2):
    x = np.ascontiguousarray(np.asarray(L0_attn_x, dtype=np.float32))
    shared = _host_layouts(L0_attn_Wq_down, L0_attn_Wkv_down, L0_attn_Wq_up,
                           L0_attn_Wk_up, L0_attn_Wv_up, L0_attn_Wo,
                           L0_attn_bo, L0_ffn_router_w, L0_ffn_w1, L0_ffn_b1,
                           L0_ffn_w2, L0_ffn_b2)
    in_maps = []
    for cid in range(8):
        b, half = cid // 2, cid % 2
        m = dict(shared)
        m["x"] = np.ascontiguousarray(x[b, half * T:(half + 1) * T, :])
        sel = np.zeros((64, 16), np.float32)
        for cp in range(8):
            for e in range(8):
                if cp < cid:
                    sel[cp * 8 + e, e] = 1.0
                sel[cp * 8 + e, 8 + e] = 1.0
        m["selmats"] = sel
        m["wqkv"] = np.ascontiguousarray(shared["wqkv"][half * 8:(half + 1) * 8])
        cix = np.empty((128, 16), np.int32)
        for c in range(16):
            w_, hh = c // 8, c % 8
            cix[:, c] = (w_ * 1024 + half * 512 + (hh % 4) * 128
                         + np.arange(128))
        m["ctxidx_h"] = cix
        in_maps.append(m)
    key = N_SUBLAYERS
    if key not in _cache:
        _cache[key] = build_program(N_SUBLAYERS)
    nc = _cache[key]
    res = run_bass_kernel_spmd(nc, in_maps, core_ids=list(range(8)))
    out = np.empty((4, 2048, 2048), np.float32)
    for cid in range(8):
        b, half = cid // 2, cid % 2
        out[b, half * T:(half + 1) * T, :] = res.results[cid]["y"]
    return out
